# version 03
# baseline (speedup 1.0000x reference)
import numpy as np
import ml_dtypes
import concourse.bass as bass
import concourse.mybir as mybir
from concourse.bass_utils import run_bass_kernel_spmd

F32 = mybir.dt.float32
BF16 = mybir.dt.bfloat16
AF = mybir.ActivationFunctionType
ALU = mybir.AluOpType
AX = mybir.AxisListType
AP = bass.AP

B, T, D, H, DH = 2, 16384, 1024, 16, 64
TOK = 4096
NXT = TOK // 128
NH = 4
E, HID = 8, 4096
CH = 128
NCHUNK = T // CH
SEC = 2048
NSEC = T // SEC
CPS = SEC // CH
TW = 1024
NW = TOK // TW
QKV_ROWS = 3 * D + 16
RG = [[0, 1, 2, 3, 4, 5, 6, 7]]
NPOW = 6
N_EXPERTS = 8


class Plan:
    ENGS = ("sync", "tensor", "vector", "scalar", "gpsimd")

    def __init__(self):
        self.ops = {e: [] for e in self.ENGS}
        self.sem_names = []
        self.counts = {}
        self._cid = 0

    def sem(self, name):
        self.sem_names.append(name)
        self.counts[name] = 0
        return name

    def op(self, eng, fn, waits=(), inc=None, amt=1):
        waits = tuple(w for w in waits if w is not None)
        rec = {"eng": eng, "fn": fn, "waits": waits, "sem": inc, "amt": amt,
               "used": False, "cid": self._cid}
        self._cid += 1
        if inc is not None and amt != 1:
            self.counts[inc] += amt
        self.ops[eng].append(rec)
        return rec

    def resolve(self):
        for eng in self.ENGS:
            for rec in self.ops[eng]:
                for w in rec["waits"]:
                    if isinstance(w, dict):
                        w["used"] = True
        run = {n: 0 for n in self.sem_names}
        for eng in self.ENGS:
            pass
        owner = {}
        for eng in self.ENGS:
            for rec in self.ops[eng]:
                if rec["sem"] is not None:
                    o = owner.setdefault(rec["sem"], eng)
                    assert o == eng, f"sem {rec['sem']} multi-engine ({o},{eng})"
        for eng in self.ENGS:
            for rec in self.ops[eng]:
                sm, amt = rec["sem"], rec["amt"]
                if sm is None:
                    rec["val"] = None
                    continue
                if amt != 1 or rec["used"]:
                    rec["keep"] = True
                    run[sm] += amt
                    rec["val"] = run[sm]
                else:
                    rec["keep"] = False
                    rec["val"] = None
        for n, v in run.items():
            assert v < 60000, f"sem {n} total {v} exceeds 16-bit safety bound"
        self.final = run


def build(dbg=None):
    global _LAST_PLAN
    nc = bass.Bass("TRN2", target_bir_lowering=False)
    p = Plan()
    _LAST_PLAN = p

    x_my = nc.dram_tensor("x_my", [TOK, D], F32, kind="ExternalInput")
    wqkv = nc.dram_tensor("wqkv", [D, 3 * D], F32, kind="ExternalInput")
    wbeta = nc.dram_tensor("wbeta", [D, 16], F32, kind="ExternalInput")
    b_beta_col = nc.dram_tensor("b_beta_col", [16, 1], F32, kind="ExternalInput")
    wout = nc.dram_tensor("wout", [D, D], F32, kind="ExternalInput")
    wsh = nc.dram_tensor("wsh", [D, D], F32, kind="ExternalInput")
    shb = nc.dram_tensor("shb", [1, D], F32, kind="ExternalInput")
    wgate = nc.dram_tensor("wgate", [D, E], F32, kind="ExternalInput")
    gamma = nc.dram_tensor("gamma", [1, D], F32, kind="ExternalInput")
    beta2 = nc.dram_tensor("beta2", [1, D], F32, kind="ExternalInput")
    wshard = nc.dram_tensor("wshard", [2048, HID], F32, kind="ExternalInput")
    b1_all = nc.dram_tensor("b1_all", [128, E * 32], F32, kind="ExternalInput")
    b2_all = nc.dram_tensor("b2_all", [E, D], F32, kind="ExternalInput")
    ident_in = nc.dram_tensor("ident_in", [128, 128], BF16, kind="ExternalInput")
    maskLT_in = nc.dram_tensor("maskLT_in", [128, 128], F32, kind="ExternalInput")
    maskUT_in = nc.dram_tensor("maskUT_in", [128, 128], F32, kind="ExternalInput")
    ones_in = nc.dram_tensor("ones_in", [1, 128], BF16, kind="ExternalInput")

    qkvag_in = nc.dram_tensor("qkvag_in", [QKV_ROWS, TOK], BF16, kind="Internal")
    qkvag_out = nc.dram_tensor("qkvag_out", [8 * QKV_ROWS, TOK], BF16, kind="Internal")
    wag_in = nc.dram_tensor("wag_in", [2048, HID], BF16, kind="Internal")
    wag_out = nc.dram_tensor("wag_out", [8 * 2048, HID], BF16, kind="Internal")
    oag_in = nc.dram_tensor("oag_in", [NH * DH, T], BF16, kind="Internal")
    oag_out = nc.dram_tensor("oag_out", [8 * NH * DH, T], BF16, kind="Internal")
    xnT_sp = nc.dram_tensor("xnT_sp", [D, TOK], BF16, kind="Internal")
    ffn_sp = nc.dram_tensor("ffn_sp", [TOK, D], F32, kind="Internal")
    wout_sp = nc.dram_tensor("wout_sp", [D, D], BF16, kind="Internal")
    wsh_sp = nc.dram_tensor("wsh_sp", [D, D], BF16, kind="Internal")
    rows_sp = nc.dram_tensor("rows_sp", [9, 2 * D], BF16, kind="Internal")
    qkv_my = nc.dram_tensor("qkv_my", [6 * 128, T], BF16, kind="Internal")
    beta_my = nc.dram_tensor("beta_my", [4, T], BF16, kind="Internal")
    o_my = nc.dram_tensor("o_my", [1024, TOK], BF16, kind="Internal")

    out_my = nc.dram_tensor("out_my", [TOK, D], F32, kind="ExternalOutput")

    live = {}
    order = []

    def alloc(name, shape, dt):
        cm = nc.sbuf_tensor(name, shape, dt)
        h = cm.__enter__()
        live[name] = cm
        order.append(name)
        return h

    def free(*names):
        for n in names:
            live.pop(n).__exit__(None, None, None)
            order.remove(n)

    def mark():
        return len(order)

    def free_to(mk):
        for n in reversed(order[mk:]):
            live.pop(n).__exit__(None, None, None)
        del order[mk:]

    ident = alloc("ident", [128, 128], BF16)
    maskLT = alloc("maskLT", [128, 128], F32)
    maskUT = alloc("maskUT", [128, 128], F32)
    ones128 = alloc("ones128", [1, 128], BF16)
    gam_bc = alloc("gam_bc", [128, D], F32)
    bet_bc = alloc("bet_bc", [128, D], F32)
    b1s = alloc("b1s", [128, E * 32], F32)
    cmb = alloc("cmb", [128, NXT * E], F32)
    bbcol = alloc("bbcol", [16, 1], F32)
    biasrow = alloc("biasrow", [1, 2 * D], BF16)
    stat = alloc("stat", [128, 16], F32)

    rsrc = alloc("rsrc", [1, D], F32)
    rlo = alloc("rlo", [1, D], F32)
    rhl = alloc("rhl", [1, 2 * D], BF16)

    ps_cm = [nc.psum_tensor(f"ps{i}", [128, 512], F32) for i in range(6)]
    ps = [cm.__enter__() for cm in ps_cm]
    pt_cm = [nc.psum_tensor(f"pt{i}", [128, 1024], BF16) for i in range(2)]
    pt = [cm.__enter__() for cm in pt_cm]

    s_sync = p.sem("s_sync")
    s_pe = p.sem("s_pe")
    s_v = p.sem("s_v")
    s_a = p.sem("s_a")
    s_ad = p.sem("s_ad")
    s_g = p.sem("s_g")
    s_x = [p.sem("s_x0"), p.sem("s_x1")]
    s_c = [p.sem("s_c0"), p.sem("s_c1")]
    s_sec = [p.sem("s_sec0"), p.sem("s_sec1")]
    s_w1 = p.sem("s_w1")
    s_w2 = [p.sem(f"s_w2_{i}") for i in range(3)]
    s_xw = p.sem("s_xw")
    s_ot = [p.sem("s_ot0"), p.sem("s_ot1")]
    s_fr = [p.sem("s_fr0"), p.sem("s_fr1")]
    s_xr = [p.sem("s_xr0"), p.sem("s_xr1")]
    s_br = p.sem("s_br")

    p.op("sync", lambda e: e.dma_start(ident[:, :], ident_in[:, :]), inc=s_sync, amt=16)
    p.op("sync", lambda e: e.dma_start(maskLT[:, :], maskLT_in[:, :]), inc=s_sync, amt=16)
    p.op("sync", lambda e: e.dma_start(maskUT[:, :], maskUT_in[:, :]), inc=s_sync, amt=16)
    p.op("sync", lambda e: e.dma_start(ones128[:, :], ones_in[:, :]), inc=s_sync, amt=16)
    p.op("sync", lambda e: e.dma_start(b1s[:, :], b1_all[:, :]), inc=s_sync, amt=16)
    t_bb = p.op("sync", lambda e: e.dma_start(bbcol[:, :], b_beta_col[:, :]), inc=s_sync, amt=16)
    tok_consts = t_bb

    def hilo(src_fn, dst_row, waits):
        t0 = p.op("sync", lambda e: e.dma_start(rsrc[:1, :], src_fn()), waits=waits,
                  inc=s_sync, amt=16)
        t1 = p.op("vector", lambda e: e.tensor_copy(rhl[:1, 0:D], rsrc[:1, :]),
                  waits=[t0], inc=s_v)
        t2 = p.op("vector", lambda e: e.tensor_sub(rlo[:1, :], rsrc[:1, :], rhl[:1, 0:D]),
                  waits=[t1], inc=s_v)
        t3 = p.op("vector", lambda e: e.tensor_copy(rhl[:1, D:2*D], rlo[:1, :]),
                  waits=[t2], inc=s_v)
        t4 = p.op("scalar", lambda e, dst_row=dst_row: e.dma_start(
            rows_sp[dst_row:dst_row+1, :], rhl[:1, :]), waits=[t3], inc=s_ad, amt=16)
        return t4

    prev = [tok_consts]
    for ei in range(E):
        prev = [hilo(lambda ei=ei: b2_all[ei:ei+1, :], ei, prev)]
    prev = [hilo(lambda: shb[:, :], 8, prev)]
    tok_rows_sp = prev[0]

    def bcast_row(src_fn, dst, waits):
        t0 = p.op("sync", lambda e: e.dma_start(rsrc[:1, :], src_fn()), waits=waits,
                  inc=s_sync, amt=16)
        t1 = p.op("vector", lambda e: e.tensor_copy(rhl[:1, 0:D], rsrc[:1, :]),
                  waits=[t0], inc=s_v)
        t2 = p.op("vector", lambda e: e.tensor_sub(rlo[:1, :], rsrc[:1, :], rhl[:1, 0:D]),
                  waits=[t1], inc=s_v)
        t3 = p.op("vector", lambda e: e.tensor_copy(rhl[:1, D:2*D], rlo[:1, :]),
                  waits=[t2], inc=s_v)
        out_toks = []
        for j in range(2):
            tm1 = p.op("tensor", lambda e, j=j: e.matmul(
                ps[0][:, 0:512], ones128[:1, :], rhl[:1, j*512:(j+1)*512],
                start=True, stop=False), waits=[t3] + out_toks, inc=s_pe)
            tm2 = p.op("tensor", lambda e, j=j: e.matmul(
                ps[0][:, 0:512], ones128[:1, :], rhl[:1, D + j*512: D + (j+1)*512],
                start=False, stop=True), waits=[tm1], inc=s_pe)
            tv = p.op("vector", lambda e, j=j, dst=dst: e.tensor_copy(
                dst[:, j*512:(j+1)*512], ps[0][:, 0:512]), waits=[tm2], inc=s_v)
            out_toks = [tv]
        return out_toks[0]

    t_gam = bcast_row(lambda: gamma[:, :], gam_bc, [tok_rows_sp])
    t_bet = bcast_row(lambda: beta2[:, :], bet_bc, [t_gam])
    tok_prologue = t_bet

    wqkv16 = alloc("wqkv16", [128, 8 * 3072], BF16)
    wb16 = alloc("wb16", [128, 8 * 16], BF16)
    wg16 = alloc("wg16", [128, 8 * E], BF16)
    xnT = alloc("xnT", [128, 8 * TOK], BF16)
    xring = alloc("xring", [128, 2 * D], F32)
    cast_in = alloc("cast_in", [128, 2 * 1024], F32)
    cast16 = alloc("cast16", [128, 2 * 1024], BF16)
    ln_xc = alloc("ln_xc", [128, D], F32)
    ln_sq = alloc("ln_sq", [128, D], F32)
    xn16 = alloc("xn16", [128, D], BF16)
    pstage = alloc("pstage", [128, 2 * TOK], BF16)
    gl_u = alloc("gl_u", [128, 2 * E], F32)

    cast_jobs = []
    for k in range(8):
        for j in range(3):
            cast_jobs.append((lambda k=k, j=j: wqkv[k*128:(k+1)*128, j*1024:(j+1)*1024],
                              "sbuf", lambda k=k, j=j: wqkv16[:, k*3072 + j*1024: k*3072 + (j+1)*1024], 1024))
    for k in range(8):
        cast_jobs.append((lambda k=k: wbeta[k*128:(k+1)*128, :], "sbuf",
                          lambda k=k: wb16[:, k*16:(k+1)*16], 16))
    for k in range(8):
        cast_jobs.append((lambda k=k: wgate[k*128:(k+1)*128, :], "sbuf",
                          lambda k=k: wg16[:, k*E:(k+1)*E], E))
    for k in range(8):
        cast_jobs.append((lambda k=k: wout[k*128:(k+1)*128, :], "dram",
                          (wout_sp, k * 128, 0, 1024), 1024))
    for k in range(8):
        cast_jobs.append((lambda k=k: wsh[k*128:(k+1)*128, :], "dram",
                          (wsh_sp, k * 128, 0, 1024), 1024))
    for k in range(16):
        for j in range(4):
            cast_jobs.append((lambda k=k, j=j: wshard[k*128:(k+1)*128, j*1024:(j+1)*1024],
                              "dram", (wag_in, k * 128, j * 1024, 1024), 1024))

    slot_free = [None, None]
    slot_stored = [None, None]
    wag_last = None
    qkvw_toks = []
    for i, (srcf, kind, dst, width) in enumerate(cast_jobs):
        sl = i % 2
        t_ld = p.op("sync", lambda e, srcf=srcf, sl=sl, width=width: e.dma_start(
            cast_in[:, sl*1024:sl*1024+width], srcf()),
            waits=[slot_free[sl]], inc=s_c[sl], amt=16)
        if kind == "sbuf":
            t_c = p.op("vector", lambda e, dst=dst, sl=sl, width=width: e.tensor_copy(
                dst(), cast_in[:, sl*1024:sl*1024+width]),
                waits=[t_ld, slot_stored[sl]], inc=s_v)
            slot_free[sl] = t_c
            slot_stored[sl] = t_c
            qkvw_toks.append(t_c)
        else:
            t_c = p.op("vector", lambda e, sl=sl, width=width: e.tensor_copy(
                cast16[:, sl*1024:sl*1024+width], cast_in[:, sl*1024:sl*1024+width]),
                waits=[t_ld, slot_stored[sl]], inc=s_v)
            slot_free[sl] = t_c
            dten, r0, c0, w = dst
            t_st = p.op("scalar", lambda e, dten=dten, r0=r0, c0=c0, w=w, sl=sl: e.dma_start(
                dten[r0:r0+128, c0:c0+w], cast16[:, sl*1024:sl*1024+w]),
                waits=[t_c], inc=s_ad, amt=16)
            slot_stored[sl] = t_st
            if dten is wag_in:
                wag_last = t_st
    tok_cast_sbuf = qkvw_toks[-1]
    tok_cast_all = [slot_stored[0], slot_stored[1]]

    xslot_free = [None, None]
    tr_ev = []
    tok_sq_read = None
    tok_vsum_prev = None
    prev_xn_users = []
    for i in range(NXT):
        sl = i % 2
        t_ld = p.op("sync", lambda e, i=i, sl=sl: e.dma_start(
            xring[:, sl*D:(sl+1)*D], x_my[i*128:(i+1)*128, :]),
            waits=[xslot_free[sl]], inc=s_x[sl], amt=16)
        t_ms = p.op("vector", lambda e, sl=sl: e.reduce_sum(
            stat[:, 0:1], xring[:, sl*D:(sl+1)*D], AX.X), waits=[t_ld], inc=s_v)
        t_m = p.op("vector", lambda e: e.tensor_scalar(
            stat[:, 1:2], stat[:, 0:1], 1.0 / D, None, ALU.mult), waits=[t_ms], inc=s_v)
        t_xc = p.op("vector", lambda e, sl=sl: e.tensor_scalar(
            ln_xc[:, :], xring[:, sl*D:(sl+1)*D], stat[:, 1:2], None, ALU.subtract),
            waits=[t_m, tok_sq_read], inc=s_v)
        xslot_free[sl] = t_xc
        t_sq = p.op("scalar", lambda e: e.square(ln_sq[:, :], ln_xc[:, :]),
                    waits=[t_xc, tok_vsum_prev], inc=s_a)
        tok_sq_read = t_sq
        t_vs = p.op("vector", lambda e: e.reduce_sum(stat[:, 2:3], ln_sq[:, :], AX.X),
                    waits=[t_sq], inc=s_v)
        tok_vsum_prev = t_vs
        t_ve = p.op("vector", lambda e: e.tensor_scalar(
            stat[:, 3:4], stat[:, 2:3], 1.0 / D, 1e-5, ALU.mult, ALU.add),
            waits=[t_vs], inc=s_v)
        t_sd = p.op("scalar", lambda e: e.sqrt(stat[:, 4:5], stat[:, 3:4]),
                    waits=[t_ve], inc=s_a)
        t_rs = p.op("vector", lambda e: e.reciprocal(stat[:, 5:6], stat[:, 4:5]),
                    waits=[t_sd], inc=s_v)
        t_xn = p.op("vector", lambda e: e.tensor_scalar(
            xn16[:, :], ln_xc[:, :], stat[:, 5:6], None, ALU.mult),
            waits=[t_rs] + prev_xn_users, inc=s_v)
        prev_xn_users = []
        for k in range(8):
            bank = pt[k % 2]
            w_tr = [t_xn]
            if len(tr_ev) >= 2:
                w_tr.append(tr_ev[-2])
            t_tr = p.op("tensor", lambda e, k=k, bank=bank: e.transpose(
                bank[:, 0:128], xn16[:, k*128:(k+1)*128], ident[:, :]),
                waits=w_tr, inc=s_pe)
            prev_xn_users.append(t_tr)
            if k % 2 == 0:
                t_ev = p.op("vector", lambda e, k=k, i=i, bank=bank: e.tensor_copy(
                    xnT[:, k*TOK + i*128: k*TOK + (i+1)*128], bank[:, 0:128]),
                    waits=[t_tr], inc=s_v)
            else:
                t_ev = p.op("scalar", lambda e, k=k, i=i, bank=bank: e.copy(
                    xnT[:, k*TOK + i*128: k*TOK + (i+1)*128], bank[:, 0:128]),
                    waits=[t_tr], inc=s_a)
            tr_ev.append(t_ev)
        prev_xn_users = prev_xn_users[-2:]
    tok_xnT = [tr_ev[-1], tr_ev[-2]]

    xnT_sp_toks = []
    for k in range(8):
        t_sp = p.op("scalar", lambda e, k=k: e.dma_start(
            xnT_sp[k*128:(k+1)*128, :], xnT[:, k*TOK:(k+1)*TOK]),
            waits=tok_xnT if k == 0 else (), inc=s_ad, amt=16)
        xnT_sp_toks.append(t_sp)
    tok_xnT_sp = xnT_sp_toks[-1]

    bank_ev = {}
    stage_tok = [None, None]
    proj_stores = []
    first_w = [tok_cast_sbuf] + tok_xnT
    for m in range(25):
        sl = m % 2
        ev_toks = []
        for c in range(8):
            bi = c % 4
            bank = ps[bi]
            w0 = list(first_w) if (m == 0 and c == 0) else []
            if bi in bank_ev:
                w0.append(bank_ev[bi])
            if c == 0:
                w0.append(stage_tok[sl])
            for k in range(8):
                if m < 24:
                    lhs = wqkv16[:, k*3072 + m*128: k*3072 + (m+1)*128]
                    rows = 128
                else:
                    lhs = wb16[:, k*16:(k+1)*16]
                    rows = 16
                t_mm = p.op("tensor", lambda e, lhs=lhs, rows=rows, c=c, k=k, bank=bank: e.matmul(
                    bank[0:rows, 0:512], lhs, xnT[:, k*TOK + c*512: k*TOK + (c+1)*512],
                    start=(k == 0), stop=(k == 7)),
                    waits=w0 if k == 0 else (), inc=s_pe)
                w0 = []
            rows = 128 if m < 24 else 16
            if m == 24:
                t_ev = p.op("vector", lambda e, sl=sl, c=c, bank=bank: e.tensor_scalar(
                    pstage[0:16, sl*TOK + c*512: sl*TOK + (c+1)*512],
                    bank[0:16, 0:512], bbcol[0:16, 0:1], None, ALU.add),
                    waits=[t_mm, stage_tok[sl] if c == 1 else None], inc=s_v)
            elif c % 2 == 0:
                t_ev = p.op("vector", lambda e, sl=sl, c=c, bank=bank: e.tensor_copy(
                    pstage[:, sl*TOK + c*512: sl*TOK + (c+1)*512], bank[:, 0:512]),
                    waits=[t_mm], inc=s_v)
            else:
                t_ev = p.op("scalar", lambda e, sl=sl, c=c, bank=bank: e.copy(
                    pstage[:, sl*TOK + c*512: sl*TOK + (c+1)*512], bank[:, 0:512]),
                    waits=[t_mm, stage_tok[sl] if c == 1 else None], inc=s_a)
            bank_ev[bi] = t_ev
            ev_toks.append(t_ev)
        rows = 128 if m < 24 else 16
        r0 = m * 128 if m < 24 else 3072
        t_st = p.op("scalar", lambda e, sl=sl, rows=rows, r0=r0: e.dma_start(
            qkvag_in[r0:r0+rows, :], pstage[0:rows, sl*TOK:(sl+1)*TOK]),
            waits=ev_toks, inc=s_ad, amt=16)
        stage_tok[sl] = t_st
        proj_stores.append(t_st)

    g_prev = None
    gate_toks = []
    for i in range(NXT):
        w0 = [tok_cast_sbuf] + tok_xnT if i == 0 else []
        if g_prev is not None:
            w0.append(g_prev)
        for k in range(8):
            t_mm = p.op("tensor", lambda e, i=i, k=k: e.matmul(
                ps[4][0:128, 0:E], xnT[:, k*TOK + i*128: k*TOK + (i+1)*128],
                wg16[:, k*E:(k+1)*E], start=(k == 0), stop=(k == 7)),
                waits=w0 if k == 0 else (), inc=s_pe)
            w0 = []
        t1 = p.op("vector", lambda e: e.tensor_reduce(
            stat[:, 8:9], ps[4][0:128, 0:E], AX.X, ALU.max, negate=True),
            waits=[t_mm], inc=s_v)
        t2 = p.op("scalar", lambda e: e.activation(
            gl_u[:, 0:E], ps[4][0:128, 0:E], AF.Exp, bias=stat[:, 8:9]),
            waits=[t1], inc=s_a)
        g_prev = t2
        t3 = p.op("vector", lambda e: e.tensor_scalar(
            gl_u[:, E:2*E], gl_u[:, 0:E], 1.0, -2.0, ALU.is_ge, ALU.mult),
            waits=[t2], inc=s_v)
        t4 = p.op("vector", lambda e: e.tensor_add(
            gl_u[:, E:2*E], gl_u[:, E:2*E], gl_u[:, 0:E]), waits=[t3], inc=s_v)
        t5 = p.op("vector", lambda e: e.tensor_reduce(
            stat[:, 9:10], gl_u[:, E:2*E], AX.X, ALU.max), waits=[t4], inc=s_v)
        t6 = p.op("vector", lambda e: e.tensor_scalar(
            stat[:, 10:11], stat[:, 9:10], 1.0, None, ALU.add), waits=[t5], inc=s_v)
        t7 = p.op("vector", lambda e: e.reciprocal(
            stat[:, 11:12], stat[:, 10:11]), waits=[t6], inc=s_v)
        t8 = p.op("vector", lambda e: e.tensor_scalar(
            gl_u[:, E:2*E], gl_u[:, 0:E], stat[:, 9:10], None, ALU.is_ge),
            waits=[t7], inc=s_v)
        t9 = p.op("vector", lambda e: e.tensor_mul(
            gl_u[:, E:2*E], gl_u[:, E:2*E], gl_u[:, 0:E]), waits=[t8], inc=s_v)
        t10 = p.op("vector", lambda e, i=i: e.tensor_scalar(
            cmb[:, i*E:(i+1)*E], gl_u[:, E:2*E], stat[:, 11:12], None, ALU.mult),
            waits=[t9], inc=s_v)
        gate_toks.append(t10)
    tok_gate = gate_toks[-1]

    tok_qkv_ag = p.op("gpsimd", lambda e: e.collective_compute(
        "AllGather", ALU.bypass, replica_groups=RG,
        ins=[qkvag_in.ap().opt()], outs=[qkvag_out.ap().opt()]),
        waits=proj_stores[-2:], inc=s_g)
    tok_w_ag = p.op("gpsimd", lambda e: e.collective_compute(
        "AllGather", ALU.bypass, replica_groups=RG,
        ins=[wag_in.ap().opt()], outs=[wag_out.ap().opt()]),
        waits=[wag_last], inc=s_g)

    P1_DONE = [tok_gate, tok_xnT_sp, proj_stores[-1], tok_qkv_ag, tok_w_ag,
               tok_prologue, tok_rows_sp] + tok_cast_all

    if dbg == "p1":
        _dbg_p1(nc, p, locals())
        return nc

    dynregs = {}

    def dynsetup(e):
        rq = e.alloc_register("r_dynq")
        rb = e.alloc_register("r_dynb")
        ro = e.alloc_register("r_dyno")
        rt = e.alloc_register("r_dynt")
        pid = e.partition_id()
        e.reg_alu(rq, pid, 4, ALU.divide)
        e.reg_alu(rq, rq, 4 * QKV_ROWS * TOK, ALU.mult)
        e.reg_alu(rt, pid, 4, ALU.mod)
        e.reg_alu(rt, rt, 256 * TOK, ALU.mult)
        e.reg_alu(rq, rq, rt, ALU.add)
        e.reg_alu(rb, pid, 4, ALU.divide)
        e.reg_alu(rb, rb, 4 * QKV_ROWS * TOK, ALU.mult)
        e.reg_alu(rt, pid, 4, ALU.mod)
        e.reg_alu(rt, rt, 4 * TOK, ALU.mult)
        e.reg_alu(rb, rb, rt, ALU.add)
        e.reg_alu(rb, rb, 3072 * TOK, ALU.add)
        e.reg_alu(ro, pid, 4, ALU.divide)
        e.reg_alu(ro, ro, 1024 * T, ALU.mult)
        e.reg_alu(rt, pid, 4, ALU.mod)
        e.reg_alu(rt, rt, TOK, ALU.mult)
        e.reg_alu(ro, ro, rt, ALU.add)
        dynregs.update(q=rq, b=rb, o=ro, t=rt)
        return e.nop()

    p.op("gpsimd", dynsetup, waits=[tok_qkv_ag, tok_w_ag])
    consol_toks = []
    for sel in range(3):
        for half in range(2):
            off = (sel * 1024 + half * 128) * TOK

            def cfn(e, sel=sel, half=half, off=off):
                e.reg_alu(dynregs["t"], dynregs["q"], off, ALU.add)
                return e.dma_start(
                    qkv_my[(sel * 2 + half) * 128:(sel * 2 + half + 1) * 128, :],
                    AP(qkvag_out, dynregs["t"],
                       [[TOK, 128], [QKV_ROWS * TOK, 4], [1, TOK]]))
            consol_toks.append(p.op("gpsimd", cfn, inc=s_g, amt=16))

    def cbeta(e):
        return e.dma_start(
            beta_my[:, :],
            AP(qkvag_out, dynregs["b"], [[TOK, 4], [QKV_ROWS * TOK, 4], [1, TOK]]))
    consol_toks.append(p.op("gpsimd", cbeta, inc=s_g, amt=16))
    tok_consol = consol_toks[-1]


    free("gl_u", "pstage", "xn16", "ln_sq", "ln_xc", "cast16", "cast_in",
         "xring", "xnT", "wg16", "wb16", "wqkv16", "rhl", "rlo", "rsrc")
    mk_p2 = mark()
    pt_cm[1].__exit__(None, None, None)
    pt_cm[0].__exit__(None, None, None)
    pf_cm = [nc.psum_tensor(f"pf{i}", [128, 512], F32) for i in range(2)]
    pf = [cm.__enter__() for cm in pf_cm]
    BANKS = ps + pf

    xnw = alloc("xnw", [128, 8 * TW], BF16)
    w1buf = alloc("w1buf", [128, 8 * HID], BF16)
    w2r = alloc("w2r", [128, 3 * 1024], BF16)
    hT = alloc("hT", [128, 32 * TW], BF16)
    outw = alloc("outw", [128, 8 * TW], F32)
    gst = alloc("gst", [128, 512], F32)
    lnq = alloc("lnq", [128, TW], F32)

    bank_ev2 = {i: None for i in range(8)}
    gst_free = [None]
    tok_hT_users = None
    spill_toks = [None] * NW
    tok_xnw_users = None
    tok_w1_users = None
    tok_biasrow_users = None
    tok_outw_spill = None
    w1_seq = [0]
    tok_w1_loaded = None

    def load_w1(ei, waits):
        toks = []
        for k in range(8):
            t = p.op("sync", lambda e, ei=ei, k=k: e.dma_start(
                w1buf[:, k*HID:(k+1)*HID],
                wag_out[2048*ei + k*128: 2048*ei + (k+1)*128, :]),
                waits=waits if k == 0 else (), inc=s_w1, amt=16)
            toks.append(t)
        return toks[-1]

    tok_w1_loaded = load_w1(0, P1_DONE + [tok_w_ag, tok_consol])
    xnw_toks = []
    for k in range(8):
        t = p.op("sync", lambda e, k=k: e.dma_start(
            xnw[:, k*TW:(k+1)*TW], xnT_sp[k*128:(k+1)*128, 0:TW]),
            waits=([tok_xnT_sp] + P1_DONE) if k == 0 else (), inc=s_xw, amt=16)
        xnw_toks.append(t)
    tok_xnw_loaded = xnw_toks[-1]

    for w in range(NW):
        wsh_toks = []
        for k in range(8):
            wa = []
            if k == 0:
                wa = [tok_xnw_loaded]
                if tok_outw_spill is not None:
                    wa.append(tok_outw_spill)
                if tok_hT_users is not None:
                    wa.append(tok_hT_users)
            t = p.op("sync", lambda e, k=k: e.dma_start(
                hT[:, (24 + k) * TW: (24 + k) * TW + 1024],
                wsh_sp[k*128:(k+1)*128, :]), waits=wa, inc=s_w2[0], amt=16)
            wsh_toks.append(t)
        t_shb = p.op("sync", lambda e: e.dma_start(biasrow[:1, :], rows_sp[8:9, :]),
                     waits=[tok_biasrow_users], inc=s_br, amt=16)
        sh_mm_last = None
        for tt in range(8):
            for ch in range(2):
                bi = (tt * 2 + ch) % 4
                bank = BANKS[bi]
                w0 = [wsh_toks[-1], t_shb, bank_ev2[bi]]
                last = None
                for k in range(8):
                    last = p.op("tensor", lambda e, k=k, tt=tt, ch=ch, bank=bank: e.matmul(
                        bank[:, 0:512], xnw[:, k*TW + tt*128: k*TW + (tt+1)*128],
                        hT[:, (24+k)*TW + ch*512: (24+k)*TW + (ch+1)*512],
                        start=(k == 0), stop=False),
                        waits=w0 if k == 0 else (), inc=s_pe)
                    w0 = []
                last = p.op("tensor", lambda e, ch=ch, bank=bank: e.matmul(
                    bank[:, 0:512], ones128[:1, :], biasrow[:1, ch*512:(ch+1)*512],
                    start=False, stop=False), inc=s_pe)
                last = p.op("tensor", lambda e, ch=ch, bank=bank: e.matmul(
                    bank[:, 0:512], ones128[:1, :], biasrow[:1, D + ch*512: D + (ch+1)*512],
                    start=False, stop=True), inc=s_pe)
                sh_mm_last = last
                t_g = p.op("scalar", lambda e, bank=bank: e.activation(
                    gst[:, :], bank[:, 0:512], AF.Gelu),
                    waits=[last, gst_free[0]], inc=s_a)
                t_o = p.op("vector", lambda e, tt=tt, ch=ch: e.tensor_scalar(
                    outw[:, tt*TW + ch*512: tt*TW + (ch+1)*512], gst[:, :],
                    0.5, None, ALU.mult), waits=[t_g], inc=s_v)
                gst_free[0] = t_o
                bank_ev2[bi] = t_o
        tok_biasrow_users = sh_mm_last

        for ei in range(N_EXPERTS):
            mm1_last = None
            for ht in range(32):
                for ch in range(2):
                    bi = (ht * 2 + ch) % 4
                    bank = BANKS[bi]
                    w0 = [bank_ev2[bi]]
                    if ht == 0 and ch == 0:
                        w0.append(tok_w1_loaded)
                    last = None
                    for k in range(8):
                        last = p.op("tensor", lambda e, k=k, ht=ht, ch=ch, bank=bank: e.matmul(
                            bank[:, 0:512],
                            w1buf[:, k*HID + ht*128: k*HID + (ht+1)*128],
                            xnw[:, k*TW + ch*512: k*TW + (ch+1)*512],
                            start=(k == 0), stop=(k == 7)),
                            waits=w0 if k == 0 else (), inc=s_pe)
                        w0 = []
                    mm1_last = last
                    t_h = p.op("scalar", lambda e, ei=ei, ht=ht, ch=ch, bank=bank: e.activation(
                        hT[:, ht*TW + ch*512: ht*TW + (ch+1)*512], bank[:, 0:512],
                        AF.Gelu, bias=b1s[:, ei*32+ht: ei*32+ht+1]),
                        waits=[last], inc=s_a)
                    bank_ev2[bi] = t_h
            nxt = ei + 1
            if not (w == NW - 1 and ei == N_EXPERTS - 1):
                tok_w1_loaded = load_w1(nxt % N_EXPERTS, [mm1_last])
            tok_w1_users = mm1_last
            t_b2 = p.op("sync", lambda e, ei=ei: e.dma_start(
                biasrow[:1, :], rows_sp[ei:ei+1, :]),
                waits=[tok_biasrow_users], inc=s_br, amt=16)
            mm2_last = None
            ring_cons = {}
            def load_w2(hk, wa):
                slot = hk % 3
                off = (2048 * ei + 1024 + hk * 32) * HID
                return p.op("sync", lambda e, off=off, slot=slot: e.dma_start(
                    w2r[:, slot*1024:(slot+1)*1024],
                    AP(wag_out, off, [[1024, 128], [1, 1024]])),
                    waits=wa, inc=s_w2[slot], amt=16)
            for half in range(2):
                ring_toks = {}
                for hk in range(3):
                    ring_toks[hk] = load_w2(hk, [ring_cons.get(hk % 3)])
                for hk in range(32):
                    slot = hk % 3
                    last = None
                    for tt in range(4):
                        for ch in range(2):
                            bi = tt * 2 + ch
                            bank = BANKS[bi]
                            w0 = []
                            if tt == 0 and ch == 0:
                                w0.append(ring_toks[hk])
                            if hk == 0:
                                w0.append(bank_ev2[bi])
                            last = p.op("tensor", lambda e, hk=hk, tt=tt, ch=ch, half=half, bank=bank, slot=slot: e.matmul(
                                bank[:, 0:512],
                                hT[:, hk*TW + half*512 + tt*128: hk*TW + half*512 + (tt+1)*128],
                                w2r[:, slot*1024 + ch*512: slot*1024 + (ch+1)*512],
                                start=(hk == 0), stop=False),
                                waits=w0, inc=s_pe)
                    ring_cons[slot] = last
                    if hk + 3 < 32:
                        ring_toks[hk + 3] = load_w2(hk + 3, [last if hk == 29 else ring_cons.get((hk + 3) % 3)])
                for tt in range(4):
                    for ch in range(2):
                        bi = tt * 2 + ch
                        bank = BANKS[bi]
                        last = p.op("tensor", lambda e, ch=ch, bank=bank: e.matmul(
                            bank[:, 0:512], ones128[:1, :],
                            biasrow[:1, ch*512:(ch+1)*512],
                            start=False, stop=False), waits=[t_b2] if tt == 0 and ch == 0 else (), inc=s_pe)
                        last = p.op("tensor", lambda e, ch=ch, bank=bank: e.matmul(
                            bank[:, 0:512], ones128[:1, :],
                            biasrow[:1, D + ch*512: D + (ch+1)*512],
                            start=False, stop=True), inc=s_pe)
                        mm2_last = last
                        gtt = half * 4 + tt
                        ccol = (w * 8 + gtt) * E + ei
                        t_e = p.op("vector", lambda e, gtt=gtt, ch=ch, ccol=ccol, bank=bank: e.scalar_tensor_tensor(
                            outw[:, gtt*TW + ch*512: gtt*TW + (ch+1)*512],
                            bank[:, 0:512], cmb[:, ccol:ccol+1],
                            outw[:, gtt*TW + ch*512: gtt*TW + (ch+1)*512],
                            ALU.mult, ALU.add), waits=[last], inc=s_v)
                        bank_ev2[bi] = t_e
            tok_biasrow_users = mm2_last
            tok_hT_users = mm2_last

        ln_last = None
        for tt in range(8):
            ow = lambda a, b: outw[:, tt*TW + a: tt*TW + b]
            t1 = p.op("vector", lambda e, tt=tt: e.reduce_sum(
                stat[:, 0:1], outw[:, tt*TW:(tt+1)*TW], AX.X),
                waits=[bank_ev2[(tt*2) % 8], bank_ev2[(tt*2+1) % 8]], inc=s_v)
            t2 = p.op("vector", lambda e: e.tensor_scalar(
                stat[:, 1:2], stat[:, 0:1], 1.0 / D, None, ALU.mult), waits=[t1], inc=s_v)
            t3 = p.op("vector", lambda e, tt=tt: e.tensor_scalar(
                outw[:, tt*TW:(tt+1)*TW], outw[:, tt*TW:(tt+1)*TW],
                stat[:, 1:2], None, ALU.subtract), waits=[t2], inc=s_v)
            t4 = p.op("scalar", lambda e, tt=tt: e.square(
                lnq[:, :], outw[:, tt*TW:(tt+1)*TW]), waits=[t3], inc=s_a)
            t5 = p.op("vector", lambda e: e.reduce_sum(stat[:, 2:3], lnq[:, :], AX.X),
                      waits=[t4], inc=s_v)
            t6 = p.op("vector", lambda e: e.tensor_scalar(
                stat[:, 3:4], stat[:, 2:3], 1.0 / D, 1e-5, ALU.mult, ALU.add),
                waits=[t5], inc=s_v)
            t7 = p.op("scalar", lambda e: e.sqrt(stat[:, 4:5], stat[:, 3:4]),
                      waits=[t6], inc=s_a)
            t8 = p.op("vector", lambda e: e.reciprocal(stat[:, 5:6], stat[:, 4:5]),
                      waits=[t7], inc=s_v)
            t9 = p.op("vector", lambda e, tt=tt: e.tensor_scalar(
                outw[:, tt*TW:(tt+1)*TW], outw[:, tt*TW:(tt+1)*TW],
                stat[:, 5:6], None, ALU.mult), waits=[t8], inc=s_v)
            t10 = p.op("vector", lambda e, tt=tt: e.tensor_mul(
                outw[:, tt*TW:(tt+1)*TW], outw[:, tt*TW:(tt+1)*TW], gam_bc[:, :]),
                waits=[t9], inc=s_v)
            t11 = p.op("vector", lambda e, tt=tt: e.tensor_add(
                outw[:, tt*TW:(tt+1)*TW], outw[:, tt*TW:(tt+1)*TW], bet_bc[:, :]),
                waits=[t10], inc=s_v)
            ln_last = t11
        t_sp = p.op("scalar", lambda e, w=w: e.dma_start(
            ffn_sp[w*TW:(w+1)*TW, :].rearrange("(t p) d -> p t d", p=128),
            outw[:, :].rearrange("p (t d) -> p t d", t=8)),
            waits=[ln_last], inc=s_ad, amt=16)
        spill_toks[w] = t_sp
        tok_outw_spill = t_sp
        if w + 1 < NW:
            xnw_toks = []
            for k in range(8):
                wa = []
                if k == 0:
                    wa = [mm2_last]
                t = p.op("sync", lambda e, k=k, w=w: e.dma_start(
                    xnw[:, k*TW:(k+1)*TW],
                    xnT_sp[k*128:(k+1)*128, (w+1)*TW:(w+2)*TW]),
                    waits=wa, inc=s_xw, amt=16)
                xnw_toks.append(t)
            tok_xnw_loaded = xnw_toks[-1]

    FFN_DONE = [spill_toks[-1]]
    free("lnq", "gst", "outw", "hT", "w2r", "w1buf", "xnw")
    pf_cm[1].__exit__(None, None, None)
    pf_cm[0].__exit__(None, None, None)
    pt_cm2 = [nc.psum_tensor(f"pt{i}b", [128, 1024], BF16) for i in range(2)]
    pt[0] = pt_cm2[0].__enter__()
    pt[1] = pt_cm2[1].__enter__()

    qsec = [alloc(f"qsec{t}", [128, 2 * SEC], BF16) for t in range(2)]
    ksec = [alloc(f"ksec{t}", [128, 2 * SEC], BF16) for t in range(2)]
    vsec = [alloc(f"vsec{t}", [128, 2 * SEC], BF16) for t in range(2)]
    bsec = alloc("bsec", [1, 4 * 2 * SEC], BF16)
    otsec = [alloc(f"otsec{t}", [128, SEC], BF16) for t in range(2)]
    knT_t = [alloc(f"knTt{h}", [128, 128], BF16) for h in range(2)]
    st16_t = [alloc(f"st16t{h}", [128, DH], BF16) for h in range(2)]
    st32_t = [alloc(f"st32t{h}", [128, DH], F32) for h in range(2)]
    wk = {}
    for j in range(4):
        pb = 64 * (j % 2)
        wk[j] = dict(
            k16=alloc(f"k16_{j}", [128, DH], BF16),
            kn=alloc(f"kn_{j}", [128, DH], BF16),
            knT=knT_t[j // 2][pb:pb+64, :],
            v16=alloc(f"v16_{j}", [128, DH], BF16),
            L=alloc(f"L_{j}", [128, 128], BF16),
            P=[alloc(f"P{k}_{j}", [128, 128], BF16) for k in range(NPOW + 1)],
            R=alloc(f"R_{j}", [128, 2 * 128], BF16),
            y=alloc(f"y_{j}", [128, DH], BF16),
            u=alloc(f"u_{j}", [128, DH], BF16),
            at=alloc(f"at_{j}", [128, 128], BF16),
            o16=alloc(f"o16_{j}", [128, DH], BF16),
            st32=st32_t[j // 2][pb:pb+64, :],
            st16=st16_t[j // 2][pb:pb+64, :],
            bv=alloc(f"bv_{j}", [128, 1], F32),
            sq=alloc(f"sq_{j}", [128, DH], F32),
        )

    st_init = []
    for j in range(4):
        t1 = p.op("vector", lambda e, j=j: e.memset(wk[j]["st32"], 0.0),
                  waits=(P1_DONE + FFN_DONE) if j == 0 else (), inc=s_v)
        t2 = p.op("vector", lambda e, j=j: e.memset(wk[j]["st16"], 0.0),
                  waits=[t1], inc=s_v)
        st_init.append(t2)
    tok_p2_start = st_init[-1]

    sec_cons = {}
    pair_state = {j: {"st": st_init[j], "stq": None} for j in range(4)}
    secbuf = {0: None, 1: None, 2: None}

    wdone_waits = [("s_w1", p.counts[s_w1])] + [
        (f"s_w2_{i}", p.counts[s_w2[i]]) for i in range(3)] + [
        ("s_xw", p.counts[s_xw]), ("s_br", p.counts[s_br])]

    def emit_section_loads(s):
        sl = s % 2
        waits = list(consol_toks) + list(FFN_DONE) + wdone_waits
        if s >= 2:
            waits = [sec_cons[(jj, s - 2)] for jj in range(4)]
        tens3 = (qsec, ksec, vsec)
        toks = []
        for sel in range(3):
            for half in range(2):
                t = p.op("sync", lambda e, sel=sel, half=half, s=s, sl=sl: e.dma_start(
                    tens3[sel][half][:, sl*SEC:(sl+1)*SEC],
                    qkv_my[(sel*2+half)*128:(sel*2+half+1)*128, s*SEC:(s+1)*SEC]),
                    waits=waits if (sel == 0 and half == 0) else (),
                    inc=s_sec[sl], amt=16)
                toks.append(t)
        for j in range(4):
            t = p.op("sync", lambda e, j=j, s=s, sl=sl: e.dma_start(
                bsec[:1, (j * 2 + sl) * SEC:(j * 2 + sl + 1) * SEC],
                beta_my[j:j+1, s*SEC:(s+1)*SEC]), inc=s_sec[sl], amt=16)
            toks.append(t)
        return toks

    PAIR_BANKS = {0: [0, 1], 1: [2, 3], 2: [4], 3: [5]}
    ptev = {0: bank_ev2[6], 1: bank_ev2[7]}
    psev = {i: bank_ev2[i] for i in range(6)}
    ps_rr = {j: 0 for j in range(4)}

    def tr_op(j, in_ap, rI, cI, evac_fn, eng2, waits=(), bp=0):
        bk = j // 2
        bank = pt[bk]
        w = list(waits)
        if ptev[bk] is not None:
            w.append(ptev[bk])
        t_tr = p.op("tensor", lambda e, in_ap=in_ap, bank=bank, rI=rI, cI=cI, bp=bp: e.transpose(
            bank[0:cI, 0:rI], in_ap, ident[bp:bp+rI, bp:bp+rI]), waits=w, inc=s_pe)
        t_ev = p.op(eng2, lambda e, bank=bank, rI=rI, cI=cI, evac_fn=evac_fn: evac_fn(
            e, bank[0:cI, 0:rI]), waits=[t_tr], inc=s_v if eng2 == "vector" else s_a)
        ptev[bk] = t_ev
        return t_ev

    def ps_op(j, mm_fns, evac_fn, eng2, waits=()):
        blist = PAIR_BANKS[j]
        bi = blist[ps_rr[j] % len(blist)]
        ps_rr[j] += 1
        bank = ps[bi]
        w = list(waits)
        if psev[bi] is not None:
            w.append(psev[bi])
        last = None
        for ix, f in enumerate(mm_fns):
            last = p.op("tensor", lambda e, f=f, bank=bank: f(e, bank),
                        waits=w if ix == 0 else (), inc=s_pe)
        t_ev = p.op(eng2, lambda e, bank=bank, evac_fn=evac_fn: evac_fn(e, bank),
                    waits=[last], inc=s_v if eng2 == "vector" else s_a)
        psev[bi] = t_ev
        return t_ev, last

    ot_store_toks = []
    ot_sec_last = {0: None, 1: None}
    ot_written = {0: None, 1: None}

    def chunk_gen(ci, j, sl, sec_tok_count):
        W = wk[j]
        cc = (ci % CPS) * CH
        qco = sl * SEC + cc
        sec_wait = (s_sec[sl], sec_tok_count)
        pb = 64 * (j % 2)
        tq, tk, tv = qsec[j // 2], ksec[j // 2], vsec[j // 2]
        stt = pair_state[j]

        t_k16 = tr_op(j, tk[pb:pb+64, qco:qco+CH], 64, CH,
                      lambda e, b: e.tensor_copy(W["k16"][:, :], b[0:128, 0:DH]),
                      "vector", waits=[sec_wait], bp=pb)
        t_v16 = tr_op(j, tv[pb:pb+64, qco:qco+CH], 64, CH,
                      lambda e, b: e.copy(W["v16"][:, :], b[0:128, 0:DH]),
                      "scalar", waits=[sec_wait], bp=pb)
        bcol = (j * 2 + sl) * SEC + cc
        t_bv, _ = ps_op(j, [lambda e, b, bcol=bcol: e.matmul(
            b[0:128, 0:1], bsec[:1, bcol:bcol+CH], ones128[:1, 0:1],
            start=True, stop=True)],
            lambda e, b: e.activation(W["bv"][:, :], b[0:128, 0:1], AF.Sigmoid),
            "scalar", waits=[sec_wait])
        yield
        t_sq = p.op("scalar", lambda e: e.square(W["sq"][:, :], W["k16"][:, :]),
                    waits=[t_k16], inc=s_a)
        t_ss = p.op("vector", lambda e, j=j: e.reduce_sum(
            stat[:, 12+0:13+0], W["sq"][:, :], AX.X), waits=[t_sq], inc=s_v)
        t_sd = p.op("scalar", lambda e: e.sqrt(stat[:, 13:14], stat[:, 12:13]),
                    waits=[t_ss], inc=s_a)
        t_mx = p.op("vector", lambda e: e.tensor_scalar(
            stat[:, 13:14], stat[:, 13:14], 1e-12, None, ALU.max),
            waits=[t_sd], inc=s_v)
        t_rc = p.op("vector", lambda e: e.reciprocal(stat[:, 14:15], stat[:, 13:14]),
                    waits=[t_mx], inc=s_v)
        t_kn = p.op("vector", lambda e: e.tensor_scalar(
            W["kn"][:, :], W["k16"][:, :], stat[:, 14:15], None, ALU.mult),
            waits=[t_rc], inc=s_v)
        yield
        t_knT = tr_op(j, W["kn"][:, :], 128, 64,
                      lambda e, b: e.tensor_copy(W["knT"], b[0:64, 0:128]),
                      "vector", waits=[t_kn])
        yield
        t_L, _ = ps_op(j, [lambda e, b: e.matmul(
            b[0:128, 0:128], W["knT"], W["knT"], start=True, stop=True)],
            lambda e, b: e.scalar_tensor_tensor(
                W["L"][:, :], b[0:128, 0:128], W["bv"][:, 0:1], maskLT[:, :],
                ALU.mult, ALU.mult),
            "vector", waits=[t_knT, t_bv])
        yield
        t_P0 = tr_op(j, W["L"][:, :], 128, 128,
                     lambda e, b: e.copy(W["P"][0][:, :], b[0:128, 0:128]),
                     "scalar", waits=[t_L])
        yield
        t_prevP = t_P0
        t_prevR = t_L
        prevR_ap = W["L"][:, :]
        for kq in range(1, NPOW + 1):
            rsl = kq % 2
            t_R, _ = ps_op(j, [lambda e, b, kq=kq, prevR_ap=prevR_ap: e.matmul(
                b[0:128, 0:128], W["P"][kq - 1][:, :], prevR_ap,
                start=True, stop=True)],
                lambda e, b, rsl=rsl: e.tensor_copy(
                    W["R"][:, rsl*128:(rsl+1)*128], b[0:128, 0:128]),
                "vector", waits=[t_prevP, t_prevR])
            yield
            t_P = tr_op(j, W["R"][:, rsl*128:(rsl+1)*128], 128, 128,
                        (lambda kq: lambda e, b: e.copy(
                            W["P"][kq][:, :], b[0:128, 0:128]))(kq),
                        "scalar", waits=[t_R])
            t_prevP = t_P
            t_prevR = t_R
            prevR_ap = W["R"][:, rsl*128:(rsl+1)*128]
            yield
        t_y0, _ = ps_op(j, [lambda e, b: e.matmul(
            b[0:128, 0:DH], W["knT"], W["st16"], start=True, stop=True)],
            lambda e, b: e.tensor_sub(W["sq"][:, :], W["v16"][:, :], b[0:128, 0:DH]),
            "vector", waits=[t_knT, t_v16, stt["st"]])
        t_y = p.op("vector", lambda e: e.tensor_scalar(
            W["y"][:, :], W["sq"][:, :], W["bv"][:, 0:1], None, ALU.mult),
            waits=[t_y0, t_bv], inc=s_v)
        yield
        for kq in range(NPOW, 0, -1):
            t_y, _ = ps_op(j, [(lambda kq: lambda e, b: e.matmul(
                b[0:128, 0:DH], W["P"][kq][:, :], W["y"][:, :],
                start=True, stop=True))(kq)],
                lambda e, b: e.tensor_add(W["y"][:, :], W["y"][:, :], b[0:128, 0:DH]),
                "vector", waits=[t_y])
            yield
        t_u, _ = ps_op(j, [lambda e, b: e.matmul(
            b[0:128, 0:DH], W["P"][0][:, :], W["y"][:, :], start=True, stop=True)],
            lambda e, b: e.tensor_sub(W["u"][:, :], W["y"][:, :], b[0:128, 0:DH]),
            "vector", waits=[t_y])
        yield
        t_at, _ = ps_op(j, [lambda e, b: e.matmul(
            b[0:128, 0:128], W["knT"], tq[pb:pb+64, qco:qco+CH],
            start=True, stop=True)],
            lambda e, b: e.scalar_tensor_tensor(
                W["at"][:, :], b[0:128, 0:128], 1.0, maskUT[:, :],
                ALU.mult, ALU.mult),
            "vector", waits=[t_knT, sec_wait])
        yield
        t_o, _ = ps_op(j, [
            lambda e, b: e.matmul(b[0:128, 0:DH], tq[pb:pb+64, qco:qco+CH],
                                  W["st16"], start=True, stop=False),
            lambda e, b: e.matmul(b[0:128, 0:DH], W["at"][:, :], W["u"][:, :],
                                  start=False, stop=True)],
            lambda e, b: e.copy(W["o16"][:, :], b[0:128, 0:DH]),
            "scalar", waits=[t_at, t_u, stt["st"]])
        yield
        w_ot = [t_o]
        if (ci % CPS) == 0 and ot_written[j // 2] is not None:
            w_ot.append(ot_written[j // 2])
        t_ot = tr_op(j, W["o16"][:, :], 128, 64,
                     (lambda j=j, cc=cc: lambda e, b: e.tensor_copy(
                         otsec[j // 2][64*(j % 2):64*(j % 2)+64, cc:cc+CH],
                         b[0:64, 0:128]))(),
                     "vector", waits=w_ot)
        ot_sec_last[j // 2] = t_ot
        t_stp, _ = ps_op(j, [lambda e, b: e.matmul(
            b[0:64, 0:DH], W["kn"][:, :], W["u"][:, :], start=True, stop=True)],
            lambda e, b: e.tensor_add(W["st32"], W["st32"], b[0:64, 0:DH]),
            "vector", waits=[t_u, stt["stq"]])
        t_st16 = p.op("scalar", lambda e: e.copy(W["st16"], W["st32"]),
                      waits=[t_stp, t_o], inc=s_a)
        stt["st"] = t_st16
        stt["stq"] = t_stp
        sec_cons[(j, ci // CPS)] = t_o

    emit_section_loads(0)
    emit_section_loads(1)
    for s in range(NSEC):
        sl = s % 2
        need = 16 * 10 * (s // 2 + 1)
        for cl in range(CPS):
            ci = s * CPS + cl
            gens = [chunk_gen(ci, j, sl, need) for j in range(4)]
            done = [False] * 4
            while not all(done):
                for gi, g in enumerate(gens):
                    if not done[gi]:
                        try:
                            next(g)
                        except StopIteration:
                            done[gi] = True
        for half in range(2):
            t_w = p.op("gpsimd", lambda e, half=half, s=s: e.dma_start(
                oag_in[half*128:(half+1)*128, s*SEC:(s+1)*SEC], otsec[half][:, :]),
                waits=[ot_sec_last[half]], inc=s_g, amt=16)
            ot_written[half] = t_w
            ot_store_toks.append(t_w)
        if s + 2 < NSEC:
            emit_section_loads(s + 2)
    tok_o_ag = p.op("gpsimd", lambda e: e.collective_compute(
        "AllGather", ALU.bypass, replica_groups=RG,
        ins=[oag_in.ap().opt()], outs=[oag_out.ap().opt()]),
        waits=ot_store_toks[-2:], inc=s_g)

    P2_DONE = [tok_o_ag] + [pair_state[j]["st"] for j in range(4)]

    if dbg == "p2":
        _dbg_p2(nc, p, locals())
        return nc

    def c_o(e):
        return e.dma_start(o_my[:, :], AP(oag_out, dynregs["o"], [[T, 1024], [1, TOK]]))
    tok_o_my0 = p.op("gpsimd", c_o, waits=[tok_o_ag], inc=s_g, amt=16)
    tok_o_my = p.op("gpsimd", c_o, waits=[tok_o_my0], inc=s_g, amt=16)

    free_to(mk_p2)
    for i in range(6):
        bank_ev2[i] = psev[i]
    otb = alloc("otb", [128, 2 * 8 * 512], BF16)
    wout16 = alloc("wout16", [128, 8 * 1024], BF16)
    frr = alloc("frr", [128, 2 * 1024], F32)
    xrr = alloc("xrr", [128, 2 * 1024], F32)
    ostage = alloc("ostage", [128, 1024], F32)
    s_out = p.sem("s_out")
    s_otb = [p.sem("s_otb0"), p.sem("s_otb1")]

    wout_toks = []
    for k in range(8):
        t = p.op("sync", lambda e, k=k: e.dma_start(
            wout16[:, k*1024:(k+1)*1024], wout_sp[k*128:(k+1)*128, :]),
            waits=(FFN_DONE + P2_DONE) if k == 0 else (), inc=s_w1, amt=16)
        wout_toks.append(t)
    tok_wout = wout_toks[-1]

    otb_cons = [None, None]
    xr_cons = [None, None]
    fr_cons = [None, None]
    ostage_free = [None]
    out_cnt = [0]
    tok_out_last = None
    for st in range(8):
        sl = st % 2
        ot_toks = []
        for k in range(8):
            wa = []
            if k == 0:
                wa = [tok_o_my, otb_cons[sl]]
            t = p.op("sync", lambda e, k=k, st=st, sl=sl: e.dma_start(
                otb[:, sl*4096 + k*512: sl*4096 + (k+1)*512],
                o_my[k*128:(k+1)*128, st*512:(st+1)*512]),
                waits=wa, inc=s_otb[sl], amt=16)
            ot_toks.append(t)
        for ti in range(4):
            tt = st * 4 + ti
            t_x = p.op("sync", lambda e, tt=tt, ti=ti: e.dma_start(
                xrr[:, (tt % 2)*D:(tt % 2 + 1)*D], x_my[tt*128:(tt+1)*128, :]),
                waits=[xr_cons[tt % 2]], inc=s_xr[tt % 2], amt=16)
            t_f = p.op("sync", lambda e, tt=tt: e.dma_start(
                frr[:, (tt % 2)*D:(tt % 2 + 1)*D], ffn_sp[tt*128:(tt+1)*128, :]),
                waits=[spill_toks[tt // 8], fr_cons[tt % 2]], inc=s_fr[tt % 2], amt=16)
            evs = []
            for ch in range(2):
                bi = (tt * 2 + ch) % 4
                bank = BANKS[bi]
                w0 = [ot_toks[-1], bank_ev2[bi]]
                if ti == 0 and ch == 0:
                    w0.append(tok_wout)
                last = None
                for k in range(8):
                    last = p.op("tensor", lambda e, k=k, ti=ti, ch=ch, sl=sl, bank=bank: e.matmul(
                        bank[:, 0:512],
                        otb[:, sl*4096 + k*512 + ti*128: sl*4096 + k*512 + (ti+1)*128],
                        wout16[:, k*1024 + ch*512: k*1024 + (ch+1)*512],
                        start=(k == 0), stop=(k == 7)),
                        waits=w0 if k == 0 else (), inc=s_pe)
                    w0 = []
                t_s1 = p.op("vector", lambda e, ch=ch, tt=tt, bank=bank: e.tensor_add(
                    ostage[:, ch*512:(ch+1)*512], bank[:, 0:512],
                    xrr[:, (tt % 2)*D + ch*512: (tt % 2)*D + (ch+1)*512]),
                    waits=[last, t_x, ostage_free[0]], inc=s_v)
                t_s2 = p.op("vector", lambda e, ch=ch, tt=tt: e.tensor_add(
                    ostage[:, ch*512:(ch+1)*512], ostage[:, ch*512:(ch+1)*512],
                    frr[:, (tt % 2)*D + ch*512: (tt % 2)*D + (ch+1)*512]),
                    waits=[t_f], inc=s_v)
                bank_ev2[bi] = t_s2
                evs.append(t_s2)
            xr_cons[tt % 2] = evs[-1]
            fr_cons[tt % 2] = evs[-1]
            t_w = p.op("scalar", lambda e, tt=tt: e.dma_start(
                out_my[tt*128:(tt+1)*128, :], ostage[:, :]), waits=evs, inc=s_out, amt=16)
            ostage_free[0] = t_w
            tok_out_last = t_w
        otb_cons[sl] = evs[-1]

    _finalize(nc, p, [(s_out, p.counts[s_out])])
    return nc


def _emit_ffn(nc, p, S):
    raise NotImplementedError


def _emit_p3(nc, p, S, ffn_sec):
    raise NotImplementedError


def _dbg_p1(nc, p, S):
    alloc = S["alloc"]
    S["free"]("gl_u", "pstage", "xn16", "ln_sq", "ln_xc", "cast16", "cast_in", "xring")
    s_v, s_g = S["s_v"], S["s_g"]
    qkvag_in, wag_in, xnT, cmb = S["qkvag_in"], S["wag_in"], S["xnT"], S["cmb"]
    out_my = S["out_my"]
    d_qkv = nc.dram_tensor("d_qkv", [QKV_ROWS, TOK], F32, kind="ExternalOutput")
    d_xnT = nc.dram_tensor("d_xnT", [D, TOK], F32, kind="ExternalOutput")
    d_cmb = nc.dram_tensor("d_cmb", [128, NXT * E], F32, kind="ExternalOutput")
    d_wag = nc.dram_tensor("d_wag", [2048, HID], F32, kind="ExternalOutput")
    cvt = alloc("cvt", [128, TOK], F32)
    cvt16 = alloc("cvt16", [128, TOK], BF16)
    prevs = list(S["P1_DONE"])
    for r in range(25):
        rows = 128 if r < 24 else 16
        r0 = r * 128 if r < 24 else 3072
        t1 = p.op("gpsimd", lambda e, r0=r0, rows=rows: e.dma_start(
            cvt16[0:rows, :], qkvag_in[r0:r0+rows, :]), waits=prevs, inc=s_g, amt=16)
        t2 = p.op("vector", lambda e, rows=rows: e.tensor_copy(
            cvt[0:rows, :], cvt16[0:rows, :]), waits=[t1], inc=s_v)
        t3 = p.op("gpsimd", lambda e, r0=r0, rows=rows: e.dma_start(
            d_qkv[r0:r0+rows, :], cvt[0:rows, :]), waits=[t2], inc=s_g, amt=16)
        prevs = [t3]
    for k in range(8):
        t2 = p.op("vector", lambda e, k=k: e.tensor_copy(
            cvt[:, :], S["xnT"][:, k*TOK:(k+1)*TOK]), waits=prevs, inc=s_v)
        t3 = p.op("gpsimd", lambda e, k=k: e.dma_start(
            d_xnT[k*128:(k+1)*128, :], cvt[:, :]), waits=[t2], inc=s_g, amt=16)
        prevs = [t3]
    t3 = p.op("gpsimd", lambda e: e.dma_start(d_cmb[:, :], cmb[:, :]),
              waits=prevs, inc=s_g, amt=16)
    prevs = [t3]
    for r in range(16):
        t1 = p.op("gpsimd", lambda e, r=r: e.dma_start(
            cvt16[:, 0:HID], wag_in[r*128:(r+1)*128, :]), waits=prevs, inc=s_g, amt=16)
        t2 = p.op("vector", lambda e: e.tensor_copy(
            cvt[:, 0:HID], cvt16[:, 0:HID]), waits=[t1], inc=s_v)
        t3 = p.op("gpsimd", lambda e, r=r: e.dma_start(
            d_wag[r*128:(r+1)*128, :], cvt[:, 0:HID]), waits=[t2], inc=s_g, amt=16)
        prevs = [t3]
    t_out = p.op("gpsimd", lambda e: e.dma_start(out_my[:, :], S["x_my"][:, :]),
                 waits=prevs, inc=s_g, amt=16)
    _finalize(nc, p, [t_out])


def _dbg_p2(nc, p, S):
    alloc = S["alloc"]
    s_v, s_g = S["s_v"], S["s_g"]
    oag_in = S["oag_in"]
    out_my = S["out_my"]
    d_ot = nc.dram_tensor("d_ot", [NH * DH, T], F32, kind="ExternalOutput")
    cvt = alloc("cvt2", [128, TOK], F32)
    cvt16 = alloc("cvt216", [128, TOK], BF16)
    prevs = list(S["P2_DONE"])
    for r in range(2):
        for cb in range(4):
            t1 = p.op("gpsimd", lambda e, r=r, cb=cb: e.dma_start(
                cvt16[:, :], oag_in[r*128:(r+1)*128, cb*TOK:(cb+1)*TOK]),
                waits=prevs, inc=s_g, amt=16)
            t2 = p.op("vector", lambda e: e.tensor_copy(cvt[:, :], cvt16[:, :]),
                      waits=[t1], inc=s_v)
            t3 = p.op("gpsimd", lambda e, r=r, cb=cb: e.dma_start(
                d_ot[r*128:(r+1)*128, cb*TOK:(cb+1)*TOK], cvt[:, :]),
                waits=[t2], inc=s_g, amt=16)
            prevs = [t3]
    t_out = p.op("gpsimd", lambda e: e.dma_start(out_my[:, :], S["x_my"][:, :]),
                 waits=prevs, inc=s_g, amt=16)
    _finalize(nc, p, [t_out])


def _finalize(nc, plan, final_toks):
    plan.resolve()
    with nc.Block() as block:
        sems = {}
        for name in plan.sem_names:
            cm = nc.semaphore(name)
            sems[name] = cm.__enter__()

        @block.sync
        def _(e):
            _emit_engine(plan, sems, "sync", e)

        @block.tensor
        def _(e):
            _emit_engine(plan, sems, "tensor", e)

        @block.vector
        def _(e):
            _emit_engine(plan, sems, "vector", e)

        @block.scalar
        def _(e):
            _emit_engine(plan, sems, "scalar", e)

        @block.gpsimd
        def _(e):
            _emit_engine(plan, sems, "gpsimd", e)
            for w in final_toks:
                if isinstance(w, dict):
                    e.wait_ge(sems[w["sem"]], w["val"])
                else:
                    sname, v = w
                    e.wait_ge(sems[sname], v)


def _emit_engine(plan, sems, ename, e):
    for rec in plan.ops[ename]:
        for w in rec["waits"]:
            if isinstance(w, dict):
                assert w.get("keep"), f"wait on dropped token {w['sem']}"
                e.wait_ge(sems[w["sem"]], w["val"])
            else:
                sname, v = w
                e.wait_ge(sems[sname], v)
        inst = rec["fn"](e)
        rec["inst"] = inst
        if rec["sem"] is not None and (rec["amt"] != 1 or rec["used"]):
            inst.then_inc(sems[rec["sem"]], rec["amt"])


def make_inmaps(inputs):
    x = np.ascontiguousarray(np.asarray(inputs["L0_attn_x"], np.float32).reshape(B * T, D))
    Wq = np.asarray(inputs["L0_attn_Wq"], np.float32)
    Wk = np.asarray(inputs["L0_attn_Wk"], np.float32)
    Wv = np.asarray(inputs["L0_attn_Wv"], np.float32)
    wqkv = np.ascontiguousarray(np.concatenate([Wq, Wk, Wv], axis=1))
    wbeta = np.ascontiguousarray(np.asarray(inputs["L0_attn_Wbeta"], np.float32))
    b_beta_col = np.ascontiguousarray(
        np.asarray(inputs["L0_attn_b_beta"], np.float32).reshape(16, 1))
    woutm = np.ascontiguousarray(np.asarray(inputs["L0_attn_Wout"], np.float32))
    W1 = np.asarray(inputs["L0_ffn_W1"], np.float32)
    b1 = np.asarray(inputs["L0_ffn_b1"], np.float32)
    W2 = np.asarray(inputs["L0_ffn_W2"], np.float32)
    b2 = np.asarray(inputs["L0_ffn_b2"], np.float32)
    wgate = np.ascontiguousarray(np.asarray(inputs["L0_ffn_gate_W"], np.float32))
    wshm = np.ascontiguousarray(np.asarray(inputs["L0_ffn_shared_W"], np.float32))
    shbm = np.asarray(inputs["L0_ffn_shared_b"], np.float32).reshape(1, D)
    gam = np.asarray(inputs["L0_ffn_gamma"], np.float32).reshape(1, D)
    bet = np.asarray(inputs["L0_ffn_beta"], np.float32).reshape(1, D)
    b1_all = np.ascontiguousarray(b1.reshape(E, 32, 128).transpose(2, 0, 1).reshape(128, E * 32))
    b2_all = np.ascontiguousarray(b2)
    ident = np.eye(128, dtype=ml_dtypes.bfloat16)
    maskLT = np.tril(np.ones((128, 128), np.float32), k=-1)
    maskUT = np.triu(np.ones((128, 128), np.float32), k=0)
    ones = np.ones((1, 128), ml_dtypes.bfloat16)
    maps = []
    for c in range(8):
        wshard = np.ascontiguousarray(
            np.concatenate([W1[c], W2[c].reshape(1024, HID)], axis=0))
        maps.append(dict(
            x_my=np.ascontiguousarray(x[c*TOK:(c+1)*TOK]),
            wqkv=wqkv, wbeta=wbeta, b_beta_col=b_beta_col, wout=woutm, wsh=wshm,
            shb=shbm, wgate=wgate, gamma=gam, beta2=bet, wshard=wshard,
            b1_all=b1_all, b2_all=b2_all,
            ident_in=ident, maskLT_in=maskLT, maskUT_in=maskUT, ones_in=ones,
        ))
    return maps


def kernel(**inputs):
    nc = build()
    maps = make_inmaps(inputs)
    res = run_bass_kernel_spmd(nc, maps, core_ids=list(range(8)))
    out = np.concatenate([res.results[c]["out_my"] for c in range(8)], axis=0)
    return out.reshape(B, T, D)


# version 04
# speedup vs baseline: 1.0809x; 1.0809x over previous
import numpy as np
import ml_dtypes
import concourse.bass as bass
import concourse.mybir as mybir
from concourse.bass_utils import run_bass_kernel_spmd

F32 = mybir.dt.float32
BF16 = mybir.dt.bfloat16
AF = mybir.ActivationFunctionType
ALU = mybir.AluOpType
AX = mybir.AxisListType
AP = bass.AP

B, T, D, H, DH = 2, 16384, 1024, 16, 64
TOK = 4096
NXT = TOK // 128
NH = 4
E, HID = 8, 4096
CH = 128
NCHUNK = T // CH
SEC = 2048
NSEC = T // SEC
CPS = SEC // CH
TW = 1024
NW = TOK // TW
QKV_ROWS = 3 * D + 16
RG = [[0, 1, 2, 3, 4, 5, 6, 7]]
NPOW = 6
N_EXPERTS = 8


class Plan:
    ENGS = ("sync", "tensor", "vector", "scalar", "gpsimd")

    def __init__(self):
        self.ops = {e: [] for e in self.ENGS}
        self.sem_names = []
        self.counts = {}
        self._cid = 0

    def sem(self, name):
        self.sem_names.append(name)
        self.counts[name] = 0
        return name

    def op(self, eng, fn, waits=(), inc=None, amt=1):
        waits = tuple(w for w in waits if w is not None)
        rec = {"eng": eng, "fn": fn, "waits": waits, "sem": inc, "amt": amt,
               "used": False, "cid": self._cid}
        self._cid += 1
        if inc is not None and amt != 1:
            self.counts[inc] += amt
        self.ops[eng].append(rec)
        return rec

    def resolve(self):
        for eng in self.ENGS:
            for rec in self.ops[eng]:
                for w in rec["waits"]:
                    if isinstance(w, dict):
                        w["used"] = True
        run = {n: 0 for n in self.sem_names}
        for eng in self.ENGS:
            pass
        owner = {}
        for eng in self.ENGS:
            for rec in self.ops[eng]:
                if rec["sem"] is not None:
                    o = owner.setdefault(rec["sem"], eng)
                    assert o == eng, f"sem {rec['sem']} multi-engine ({o},{eng})"
        for eng in self.ENGS:
            for rec in self.ops[eng]:
                sm, amt = rec["sem"], rec["amt"]
                if sm is None:
                    rec["val"] = None
                    continue
                if amt != 1 or rec["used"]:
                    rec["keep"] = True
                    run[sm] += amt
                    rec["val"] = run[sm]
                else:
                    rec["keep"] = False
                    rec["val"] = None
        for n, v in run.items():
            assert v < 60000, f"sem {n} total {v} exceeds 16-bit safety bound"
        self.final = run


def build(dbg=None):
    global _LAST_PLAN
    nc = bass.Bass("TRN2", target_bir_lowering=False)
    p = Plan()
    _LAST_PLAN = p

    x_my = nc.dram_tensor("x_my", [TOK, D], F32, kind="ExternalInput")
    wqkv = nc.dram_tensor("wqkv", [D, 3 * D], F32, kind="ExternalInput")
    wbeta = nc.dram_tensor("wbeta", [D, 16], F32, kind="ExternalInput")
    b_beta_col = nc.dram_tensor("b_beta_col", [16, 1], F32, kind="ExternalInput")
    wout = nc.dram_tensor("wout", [D, D], F32, kind="ExternalInput")
    wsh = nc.dram_tensor("wsh", [D, D], F32, kind="ExternalInput")
    shb = nc.dram_tensor("shb", [1, D], F32, kind="ExternalInput")
    wgate = nc.dram_tensor("wgate", [D, E], F32, kind="ExternalInput")
    gamma = nc.dram_tensor("gamma", [1, D], F32, kind="ExternalInput")
    beta2 = nc.dram_tensor("beta2", [1, D], F32, kind="ExternalInput")
    wshard = nc.dram_tensor("wshard", [2048, HID], F32, kind="ExternalInput")
    b1_all = nc.dram_tensor("b1_all", [128, E * 32], F32, kind="ExternalInput")
    b2_all = nc.dram_tensor("b2_all", [E, D], F32, kind="ExternalInput")
    ident_in = nc.dram_tensor("ident_in", [128, 128], BF16, kind="ExternalInput")
    maskLT_in = nc.dram_tensor("maskLT_in", [128, 128], F32, kind="ExternalInput")
    maskUT_in = nc.dram_tensor("maskUT_in", [128, 128], F32, kind="ExternalInput")
    ones_in = nc.dram_tensor("ones_in", [1, 128], BF16, kind="ExternalInput")
    idenf_in = nc.dram_tensor("idenf_in", [128, 128], F32, kind="ExternalInput")

    qkvag_in = nc.dram_tensor("qkvag_in", [QKV_ROWS, TOK], BF16, kind="Internal")
    qkvag_out = nc.dram_tensor("qkvag_out", [8 * QKV_ROWS, TOK], BF16, kind="Internal", addr_space="Shared")
    wag_in = nc.dram_tensor("wag_in", [2048, HID], BF16, kind="Internal")
    wag_out = nc.dram_tensor("wag_out", [8 * 2048, HID], BF16, kind="Internal", addr_space="Shared")
    oag_in = nc.dram_tensor("oag_in", [NH * DH, T], BF16, kind="Internal")
    oag_out = nc.dram_tensor("oag_out", [8 * NH * DH, T], BF16, kind="Internal", addr_space="Shared")
    xnT_sp = nc.dram_tensor("xnT_sp", [D, TOK], BF16, kind="Internal")
    ffn_sp = nc.dram_tensor("ffn_sp", [TOK, D], F32, kind="Internal")
    wout_sp = nc.dram_tensor("wout_sp", [D, D], BF16, kind="Internal")
    wsh_sp = nc.dram_tensor("wsh_sp", [D, D], BF16, kind="Internal")
    rows_sp = nc.dram_tensor("rows_sp", [9, 2 * D], BF16, kind="Internal")
    qkv_my = nc.dram_tensor("qkv_my", [6 * 128, T], BF16, kind="Internal")
    beta_my = nc.dram_tensor("beta_my", [4, T], BF16, kind="Internal")
    o_my = nc.dram_tensor("o_my", [1024, TOK], BF16, kind="Internal")

    out_my = nc.dram_tensor("out_my", [TOK, D], F32, kind="ExternalOutput")

    live = {}
    order = []

    def alloc(name, shape, dt):
        cm = nc.sbuf_tensor(name, shape, dt)
        h = cm.__enter__()
        live[name] = cm
        order.append(name)
        return h

    def free(*names):
        for n in names:
            live.pop(n).__exit__(None, None, None)
            order.remove(n)

    def mark():
        return len(order)

    def free_to(mk):
        for n in reversed(order[mk:]):
            live.pop(n).__exit__(None, None, None)
        del order[mk:]

    ident = alloc("ident", [128, 128], BF16)
    maskLT = alloc("maskLT", [128, 128], F32)
    maskUT = alloc("maskUT", [128, 128], F32)
    ones128 = alloc("ones128", [1, 128], BF16)
    idenf = alloc("idenf", [128, 128], F32)
    gam_bc = alloc("gam_bc", [128, D], F32)
    bet_bc = alloc("bet_bc", [128, D], F32)
    b1s = alloc("b1s", [128, E * 32], F32)
    cmb = alloc("cmb", [128, NXT * E], F32)
    bbcol = alloc("bbcol", [16, 1], F32)
    biasrow = alloc("biasrow", [1, 2 * D], BF16)
    stat = alloc("stat", [128, 32], F32)

    rsrc = alloc("rsrc", [1, D], F32)
    rlo = alloc("rlo", [1, D], F32)
    rhl = alloc("rhl", [1, 2 * D], BF16)

    ps_cm = [nc.psum_tensor(f"ps{i}", [128, 512], F32) for i in range(6)]
    ps = [cm.__enter__() for cm in ps_cm]
    pt_cm = [nc.psum_tensor(f"pt{i}", [128, 1024], BF16) for i in range(2)]
    pt = [cm.__enter__() for cm in pt_cm]

    s_sync = p.sem("s_sync")
    s_pe = p.sem("s_pe")
    s_v = p.sem("s_v")
    s_a = p.sem("s_a")
    s_ad = p.sem("s_ad")
    s_g = p.sem("s_g")
    s_x = [p.sem("s_x0"), p.sem("s_x1")]
    s_c = [p.sem("s_c0"), p.sem("s_c1")]
    s_sec = [p.sem("s_sec0"), p.sem("s_sec1")]
    s_w1 = p.sem("s_w1")
    s_w2 = [p.sem(f"s_w2_{i}") for i in range(3)]
    s_xw = p.sem("s_xw")
    s_ot = [p.sem("s_ot0"), p.sem("s_ot1")]
    s_fr = [p.sem("s_fr0"), p.sem("s_fr1")]
    s_xr = [p.sem("s_xr0"), p.sem("s_xr1")]
    s_br = p.sem("s_br")

    p.op("sync", lambda e: e.dma_start(ident[:, :], ident_in[:, :]), inc=s_sync, amt=16)
    p.op("sync", lambda e: e.dma_start(maskLT[:, :], maskLT_in[:, :]), inc=s_sync, amt=16)
    p.op("sync", lambda e: e.dma_start(maskUT[:, :], maskUT_in[:, :]), inc=s_sync, amt=16)
    p.op("sync", lambda e: e.dma_start(ones128[:, :], ones_in[:, :]), inc=s_sync, amt=16)
    p.op("sync", lambda e: e.dma_start(idenf[:, :], idenf_in[:, :]), inc=s_sync, amt=16)
    p.op("sync", lambda e: e.dma_start(b1s[:, :], b1_all[:, :]), inc=s_sync, amt=16)
    t_bb = p.op("sync", lambda e: e.dma_start(bbcol[:, :], b_beta_col[:, :]), inc=s_sync, amt=16)
    tok_consts = t_bb

    def hilo(src_fn, dst_row, waits):
        t0 = p.op("sync", lambda e: e.dma_start(rsrc[:1, :], src_fn()), waits=waits,
                  inc=s_sync, amt=16)
        t1 = p.op("vector", lambda e: e.tensor_copy(rhl[:1, 0:D], rsrc[:1, :]),
                  waits=[t0], inc=s_v)
        t2 = p.op("vector", lambda e: e.tensor_sub(rlo[:1, :], rsrc[:1, :], rhl[:1, 0:D]),
                  waits=[t1], inc=s_v)
        t3 = p.op("vector", lambda e: e.tensor_copy(rhl[:1, D:2*D], rlo[:1, :]),
                  waits=[t2], inc=s_v)
        t4 = p.op("scalar", lambda e, dst_row=dst_row: e.dma_start(
            rows_sp[dst_row:dst_row+1, :], rhl[:1, :]), waits=[t3], inc=s_ad, amt=16)
        return t4

    prev = [tok_consts]
    for ei in range(E):
        prev = [hilo(lambda ei=ei: b2_all[ei:ei+1, :], ei, prev)]
    prev = [hilo(lambda: shb[:, :], 8, prev)]
    tok_rows_sp = prev[0]

    def bcast_row(src_fn, dst, waits):
        t0 = p.op("sync", lambda e: e.dma_start(rsrc[:1, :], src_fn()), waits=waits,
                  inc=s_sync, amt=16)
        t1 = p.op("vector", lambda e: e.tensor_copy(rhl[:1, 0:D], rsrc[:1, :]),
                  waits=[t0], inc=s_v)
        t2 = p.op("vector", lambda e: e.tensor_sub(rlo[:1, :], rsrc[:1, :], rhl[:1, 0:D]),
                  waits=[t1], inc=s_v)
        t3 = p.op("vector", lambda e: e.tensor_copy(rhl[:1, D:2*D], rlo[:1, :]),
                  waits=[t2], inc=s_v)
        out_toks = []
        for j in range(2):
            tm1 = p.op("tensor", lambda e, j=j: e.matmul(
                ps[0][:, 0:512], ones128[:1, :], rhl[:1, j*512:(j+1)*512],
                start=True, stop=False), waits=[t3] + out_toks, inc=s_pe)
            tm2 = p.op("tensor", lambda e, j=j: e.matmul(
                ps[0][:, 0:512], ones128[:1, :], rhl[:1, D + j*512: D + (j+1)*512],
                start=False, stop=True), waits=[tm1], inc=s_pe)
            tv = p.op("vector", lambda e, j=j, dst=dst: e.tensor_copy(
                dst[:, j*512:(j+1)*512], ps[0][:, 0:512]), waits=[tm2], inc=s_v)
            out_toks = [tv]
        return out_toks[0]

    t_gam = bcast_row(lambda: gamma[:, :], gam_bc, [tok_rows_sp])
    t_bet = bcast_row(lambda: beta2[:, :], bet_bc, [t_gam])
    tok_prologue = t_bet

    wqkv16 = alloc("wqkv16", [128, 8 * 3072], BF16)
    wb16 = alloc("wb16", [128, 8 * 16], BF16)
    wg16 = alloc("wg16", [128, 8 * E], BF16)
    xnT = alloc("xnT", [128, 8 * TOK], BF16)
    xring = alloc("xring", [128, 2 * D], F32)
    cast_in = alloc("cast_in", [128, 2 * 1024], F32)
    cast16 = alloc("cast16", [128, 2 * 1024], BF16)
    ln_xc = alloc("ln_xc", [128, D], F32)
    ln_sq = alloc("ln_sq", [128, D], F32)
    xn16 = alloc("xn16", [128, D], BF16)
    pstage = alloc("pstage", [128, 2 * TOK], BF16)
    gl_u = alloc("gl_u", [128, 2 * E], F32)

    cast_jobs = []
    for k in range(8):
        for j in range(3):
            cast_jobs.append((lambda k=k, j=j: wqkv[k*128:(k+1)*128, j*1024:(j+1)*1024],
                              "sbuf", lambda k=k, j=j: wqkv16[:, k*3072 + j*1024: k*3072 + (j+1)*1024], 1024))
    for k in range(8):
        cast_jobs.append((lambda k=k: wbeta[k*128:(k+1)*128, :], "sbuf",
                          lambda k=k: wb16[:, k*16:(k+1)*16], 16))
    for k in range(8):
        cast_jobs.append((lambda k=k: wgate[k*128:(k+1)*128, :], "sbuf",
                          lambda k=k: wg16[:, k*E:(k+1)*E], E))
    for k in range(8):
        cast_jobs.append((lambda k=k: wout[k*128:(k+1)*128, :], "dram",
                          (wout_sp, k * 128, 0, 1024), 1024))
    for k in range(8):
        cast_jobs.append((lambda k=k: wsh[k*128:(k+1)*128, :], "dram",
                          (wsh_sp, k * 128, 0, 1024), 1024))
    for k in range(16):
        for j in range(4):
            cast_jobs.append((lambda k=k, j=j: wshard[k*128:(k+1)*128, j*1024:(j+1)*1024],
                              "dram", (wag_in, k * 128, j * 1024, 1024), 1024))

    slot_free = [None, None]
    slot_stored = [None, None]
    wag_last = None
    qkvw_toks = []
    for i, (srcf, kind, dst, width) in enumerate(cast_jobs):
        sl = i % 2
        t_ld = p.op("sync", lambda e, srcf=srcf, sl=sl, width=width: e.dma_start(
            cast_in[:, sl*1024:sl*1024+width], srcf()),
            waits=[slot_free[sl]], inc=s_c[sl], amt=16)
        if kind == "sbuf":
            t_c = p.op("vector", lambda e, dst=dst, sl=sl, width=width: e.tensor_copy(
                dst(), cast_in[:, sl*1024:sl*1024+width]),
                waits=[t_ld, slot_stored[sl]], inc=s_v)
            slot_free[sl] = t_c
            slot_stored[sl] = t_c
            qkvw_toks.append(t_c)
        else:
            t_c = p.op("vector", lambda e, sl=sl, width=width: e.tensor_copy(
                cast16[:, sl*1024:sl*1024+width], cast_in[:, sl*1024:sl*1024+width]),
                waits=[t_ld, slot_stored[sl]], inc=s_v)
            slot_free[sl] = t_c
            dten, r0, c0, w = dst
            t_st = p.op("scalar", lambda e, dten=dten, r0=r0, c0=c0, w=w, sl=sl: e.dma_start(
                dten[r0:r0+128, c0:c0+w], cast16[:, sl*1024:sl*1024+w]),
                waits=[t_c], inc=s_ad, amt=16)
            slot_stored[sl] = t_st
            if dten is wag_in:
                wag_last = t_st
    tok_cast_sbuf = qkvw_toks[-1]
    tok_cast_all = [slot_stored[0], slot_stored[1]]

    xslot_free = [None, None]
    tr_ev = []
    tok_sq_read = None
    tok_vsum_prev = None
    prev_xn_users = []
    for i in range(NXT):
        sl = i % 2
        t_ld = p.op("sync", lambda e, i=i, sl=sl: e.dma_start(
            xring[:, sl*D:(sl+1)*D], x_my[i*128:(i+1)*128, :]),
            waits=[xslot_free[sl]], inc=s_x[sl], amt=16)
        t_ms = p.op("vector", lambda e, sl=sl: e.reduce_sum(
            stat[:, 0:1], xring[:, sl*D:(sl+1)*D], AX.X), waits=[t_ld], inc=s_v)
        t_m = p.op("vector", lambda e: e.tensor_scalar(
            stat[:, 1:2], stat[:, 0:1], 1.0 / D, None, ALU.mult), waits=[t_ms], inc=s_v)
        t_xc = p.op("vector", lambda e, sl=sl: e.tensor_scalar(
            ln_xc[:, :], xring[:, sl*D:(sl+1)*D], stat[:, 1:2], None, ALU.subtract),
            waits=[t_m, tok_sq_read], inc=s_v)
        xslot_free[sl] = t_xc
        t_sq = p.op("scalar", lambda e: e.square(ln_sq[:, :], ln_xc[:, :]),
                    waits=[t_xc, tok_vsum_prev], inc=s_a)
        tok_sq_read = t_sq
        t_vs = p.op("vector", lambda e: e.reduce_sum(stat[:, 2:3], ln_sq[:, :], AX.X),
                    waits=[t_sq], inc=s_v)
        tok_vsum_prev = t_vs
        t_ve = p.op("vector", lambda e: e.tensor_scalar(
            stat[:, 3:4], stat[:, 2:3], 1.0 / D, 1e-5, ALU.mult, ALU.add),
            waits=[t_vs], inc=s_v)
        t_sd = p.op("scalar", lambda e: e.sqrt(stat[:, 4:5], stat[:, 3:4]),
                    waits=[t_ve], inc=s_a)
        t_rs = p.op("vector", lambda e: e.reciprocal(stat[:, 5:6], stat[:, 4:5]),
                    waits=[t_sd], inc=s_v)
        t_xn = p.op("vector", lambda e: e.tensor_scalar(
            xn16[:, :], ln_xc[:, :], stat[:, 5:6], None, ALU.mult),
            waits=[t_rs] + prev_xn_users, inc=s_v)
        prev_xn_users = []
        for k in range(8):
            bank = pt[k % 2]
            w_tr = [t_xn]
            if len(tr_ev) >= 2:
                w_tr.append(tr_ev[-2])
            t_tr = p.op("tensor", lambda e, k=k, bank=bank: e.transpose(
                bank[:, 0:128], xn16[:, k*128:(k+1)*128], ident[:, :]),
                waits=w_tr, inc=s_pe)
            prev_xn_users.append(t_tr)
            if k % 2 == 0:
                t_ev = p.op("vector", lambda e, k=k, i=i, bank=bank: e.tensor_copy(
                    xnT[:, k*TOK + i*128: k*TOK + (i+1)*128], bank[:, 0:128]),
                    waits=[t_tr], inc=s_v)
            else:
                t_ev = p.op("scalar", lambda e, k=k, i=i, bank=bank: e.copy(
                    xnT[:, k*TOK + i*128: k*TOK + (i+1)*128], bank[:, 0:128]),
                    waits=[t_tr], inc=s_a)
            tr_ev.append(t_ev)
        prev_xn_users = prev_xn_users[-2:]
    tok_xnT = [tr_ev[-1], tr_ev[-2]]

    xnT_sp_toks = []
    for k in range(8):
        t_sp = p.op("scalar", lambda e, k=k: e.dma_start(
            xnT_sp[k*128:(k+1)*128, :], xnT[:, k*TOK:(k+1)*TOK]),
            waits=tok_xnT if k == 0 else (), inc=s_ad, amt=16)
        xnT_sp_toks.append(t_sp)
    tok_xnT_sp = xnT_sp_toks[-1]

    bank_ev = {}
    stage_tok = [None, None]
    proj_stores = []
    first_w = [tok_cast_sbuf] + tok_xnT
    for m in range(25):
        sl = m % 2
        ev_toks = []
        for c in range(8):
            bi = c % 4
            bank = ps[bi]
            w0 = list(first_w) if (m == 0 and c == 0) else []
            if bi in bank_ev:
                w0.append(bank_ev[bi])
            if c == 0:
                w0.append(stage_tok[sl])
            for k in range(8):
                if m < 24:
                    lhs = wqkv16[:, k*3072 + m*128: k*3072 + (m+1)*128]
                    rows = 128
                else:
                    lhs = wb16[:, k*16:(k+1)*16]
                    rows = 16
                t_mm = p.op("tensor", lambda e, lhs=lhs, rows=rows, c=c, k=k, bank=bank: e.matmul(
                    bank[0:rows, 0:512], lhs, xnT[:, k*TOK + c*512: k*TOK + (c+1)*512],
                    start=(k == 0), stop=(k == 7)),
                    waits=w0 if k == 0 else (), inc=s_pe)
                w0 = []
            rows = 128 if m < 24 else 16
            if m == 24:
                t_ev = p.op("vector", lambda e, sl=sl, c=c, bank=bank: e.tensor_scalar(
                    pstage[0:16, sl*TOK + c*512: sl*TOK + (c+1)*512],
                    bank[0:16, 0:512], bbcol[0:16, 0:1], None, ALU.add),
                    waits=[t_mm, stage_tok[sl] if c == 1 else None], inc=s_v)
            elif c % 2 == 0:
                t_ev = p.op("vector", lambda e, sl=sl, c=c, bank=bank: e.tensor_copy(
                    pstage[:, sl*TOK + c*512: sl*TOK + (c+1)*512], bank[:, 0:512]),
                    waits=[t_mm], inc=s_v)
            else:
                t_ev = p.op("scalar", lambda e, sl=sl, c=c, bank=bank: e.copy(
                    pstage[:, sl*TOK + c*512: sl*TOK + (c+1)*512], bank[:, 0:512]),
                    waits=[t_mm, stage_tok[sl] if c == 1 else None], inc=s_a)
            bank_ev[bi] = t_ev
            ev_toks.append(t_ev)
        rows = 128 if m < 24 else 16
        r0 = m * 128 if m < 24 else 3072
        t_st = p.op("scalar", lambda e, sl=sl, rows=rows, r0=r0: e.dma_start(
            qkvag_in[r0:r0+rows, :], pstage[0:rows, sl*TOK:(sl+1)*TOK]),
            waits=ev_toks, inc=s_ad, amt=16)
        stage_tok[sl] = t_st
        proj_stores.append(t_st)

    g_prev = None
    gate_toks = []
    for i in range(NXT):
        w0 = [tok_cast_sbuf] + tok_xnT if i == 0 else []
        if g_prev is not None:
            w0.append(g_prev)
        for k in range(8):
            t_mm = p.op("tensor", lambda e, i=i, k=k: e.matmul(
                ps[4][0:128, 0:E], xnT[:, k*TOK + i*128: k*TOK + (i+1)*128],
                wg16[:, k*E:(k+1)*E], start=(k == 0), stop=(k == 7)),
                waits=w0 if k == 0 else (), inc=s_pe)
            w0 = []
        t1 = p.op("vector", lambda e: e.tensor_reduce(
            stat[:, 8:9], ps[4][0:128, 0:E], AX.X, ALU.max, negate=True),
            waits=[t_mm], inc=s_v)
        t2 = p.op("scalar", lambda e: e.activation(
            gl_u[:, 0:E], ps[4][0:128, 0:E], AF.Exp, bias=stat[:, 8:9]),
            waits=[t1], inc=s_a)
        g_prev = t2
        t3 = p.op("vector", lambda e: e.tensor_scalar(
            gl_u[:, E:2*E], gl_u[:, 0:E], 1.0, -2.0, ALU.is_ge, ALU.mult),
            waits=[t2], inc=s_v)
        t4 = p.op("vector", lambda e: e.tensor_add(
            gl_u[:, E:2*E], gl_u[:, E:2*E], gl_u[:, 0:E]), waits=[t3], inc=s_v)
        t5 = p.op("vector", lambda e: e.tensor_reduce(
            stat[:, 9:10], gl_u[:, E:2*E], AX.X, ALU.max), waits=[t4], inc=s_v)
        t6 = p.op("vector", lambda e: e.tensor_scalar(
            stat[:, 10:11], stat[:, 9:10], 1.0, None, ALU.add), waits=[t5], inc=s_v)
        t7 = p.op("vector", lambda e: e.reciprocal(
            stat[:, 11:12], stat[:, 10:11]), waits=[t6], inc=s_v)
        t8 = p.op("vector", lambda e: e.tensor_scalar(
            gl_u[:, E:2*E], gl_u[:, 0:E], stat[:, 9:10], None, ALU.is_ge),
            waits=[t7], inc=s_v)
        t9 = p.op("vector", lambda e: e.tensor_mul(
            gl_u[:, E:2*E], gl_u[:, E:2*E], gl_u[:, 0:E]), waits=[t8], inc=s_v)
        t10 = p.op("vector", lambda e, i=i: e.tensor_scalar(
            cmb[:, i*E:(i+1)*E], gl_u[:, E:2*E], stat[:, 11:12], None, ALU.mult),
            waits=[t9], inc=s_v)
        gate_toks.append(t10)
    tok_gate = gate_toks[-1]

    tok_w_ag = p.op("gpsimd", lambda e: e.collective_compute(
        "AllGather", ALU.bypass, replica_groups=RG,
        ins=[wag_in.ap().opt()], outs=[wag_out.ap().opt()]),
        waits=[wag_last], inc=s_g)
    tok_qkv_ag = p.op("gpsimd", lambda e: e.collective_compute(
        "AllGather", ALU.bypass, replica_groups=RG,
        ins=[qkvag_in.ap().opt()], outs=[qkvag_out.ap().opt()]),
        waits=proj_stores[-2:], inc=s_g)
    dump_sp = nc.dram_tensor("dump_sp", [2048, HID], BF16, kind="Internal")
    tok_w_settle = p.op("gpsimd", lambda e: e.dma_start(
        dump_sp[:, :], wag_out[14336:16384, :]), waits=[tok_w_ag], inc=s_g, amt=16)
    tok_qkv_settle = p.op("gpsimd", lambda e: e.dma_start(
        dump_sp[:, :], qkvag_out[7 * QKV_ROWS + 1040: 7 * QKV_ROWS + 3088, :]),
        waits=[tok_qkv_ag], inc=s_g, amt=16)

    P1_DONE = [tok_gate, tok_xnT_sp, proj_stores[-1], tok_qkv_ag, tok_w_ag,
               tok_prologue, tok_rows_sp] + tok_cast_all

    if dbg == "p1":
        _dbg_p1(nc, p, locals())
        return nc

    dynregs = {}

    def dynsetup(e):
        rq = e.alloc_register("r_dynq")
        rb = e.alloc_register("r_dynb")
        ro = e.alloc_register("r_dyno")
        rt = e.alloc_register("r_dynt")
        pid = e.partition_id()
        e.reg_alu(rq, pid, 4, ALU.divide)
        e.reg_alu(rq, rq, 4 * QKV_ROWS * TOK, ALU.mult)
        e.reg_alu(rt, pid, 4, ALU.mod)
        e.reg_alu(rt, rt, 256 * TOK, ALU.mult)
        e.reg_alu(rq, rq, rt, ALU.add)
        e.reg_alu(rb, pid, 4, ALU.divide)
        e.reg_alu(rb, rb, 4 * QKV_ROWS * TOK, ALU.mult)
        e.reg_alu(rt, pid, 4, ALU.mod)
        e.reg_alu(rt, rt, 4 * TOK, ALU.mult)
        e.reg_alu(rb, rb, rt, ALU.add)
        e.reg_alu(rb, rb, 3072 * TOK, ALU.add)
        e.reg_alu(ro, pid, 4, ALU.divide)
        e.reg_alu(ro, ro, 1024 * T, ALU.mult)
        e.reg_alu(rt, pid, 4, ALU.mod)
        e.reg_alu(rt, rt, TOK, ALU.mult)
        e.reg_alu(ro, ro, rt, ALU.add)
        dynregs.update(q=rq, b=rb, o=ro, t=rt)
        return e.nop()

    p.op("gpsimd", dynsetup, waits=[tok_qkv_settle])
    consol_toks = []
    for sel in range(3):
        for half in range(2):
            off = (sel * 1024 + half * 128) * TOK

            def cfn(e, sel=sel, half=half, off=off):
                e.reg_alu(dynregs["t"], dynregs["q"], off, ALU.add)
                return e.dma_start(
                    qkv_my[(sel * 2 + half) * 128:(sel * 2 + half + 1) * 128, :],
                    AP(qkvag_out, dynregs["t"],
                       [[TOK, 128], [QKV_ROWS * TOK, 4], [1, TOK]]))
            consol_toks.append(p.op("gpsimd", cfn, inc=s_g, amt=16))

    def cbeta(e):
        return e.dma_start(
            beta_my[:, :],
            AP(qkvag_out, dynregs["b"], [[TOK, 4], [QKV_ROWS * TOK, 4], [1, TOK]]))
    consol_toks.append(p.op("gpsimd", cbeta, inc=s_g, amt=16))
    tok_consol = consol_toks[-1]


    free("gl_u", "pstage", "xn16", "ln_sq", "ln_xc", "cast16", "cast_in",
         "xring", "xnT", "wg16", "wb16", "wqkv16", "rhl", "rlo", "rsrc")
    mk_p2 = mark()
    pt_cm[1].__exit__(None, None, None)
    pt_cm[0].__exit__(None, None, None)
    pf_cm = [nc.psum_tensor(f"pf{i}", [128, 512], F32) for i in range(2)]
    pf = [cm.__enter__() for cm in pf_cm]
    BANKS = ps + pf

    xnw = alloc("xnw", [128, 8 * TW], BF16)
    w1buf = alloc("w1buf", [128, 8 * HID], BF16)
    w2r = alloc("w2r", [128, 3 * 1024], BF16)
    hT = alloc("hT", [128, 32 * TW], BF16)
    outw = alloc("outw", [128, 8 * TW], F32)
    gst = alloc("gst", [128, 512], F32)
    lnq = alloc("lnq", [128, TW], F32)

    bank_ev2 = {i: None for i in range(8)}
    gst_free = [None]
    tok_hT_users = None
    spill_toks = [None] * NW
    tok_xnw_users = None
    tok_w1_users = None
    tok_biasrow_users = None
    tok_outw_spill = None
    w1_seq = [0]
    tok_w1_loaded = None

    def load_w1(ei, waits):
        toks = []
        for k in range(8):
            t = p.op("sync", lambda e, ei=ei, k=k: e.dma_start(
                w1buf[:, k*HID:(k+1)*HID],
                wag_out[2048*ei + k*128: 2048*ei + (k+1)*128, :]),
                waits=waits if k == 0 else (), inc=s_w1, amt=16)
            toks.append(t)
        return toks[-1]

    tok_w1_loaded = load_w1(0, P1_DONE + [tok_w_settle])
    xnw_toks = []
    for k in range(8):
        t = p.op("sync", lambda e, k=k: e.dma_start(
            xnw[:, k*TW:(k+1)*TW], xnT_sp[k*128:(k+1)*128, 0:TW]),
            waits=([tok_xnT_sp] + P1_DONE) if k == 0 else (), inc=s_xw, amt=16)
        xnw_toks.append(t)
    tok_xnw_loaded = xnw_toks[-1]

    for w in range(NW):
        wsh_toks = []
        for k in range(8):
            wa = []
            if k == 0:
                wa = [tok_xnw_loaded]
                if tok_outw_spill is not None:
                    wa.append(tok_outw_spill)
                if tok_hT_users is not None:
                    wa.append(tok_hT_users)
            t = p.op("sync", lambda e, k=k: e.dma_start(
                hT[:, (24 + k) * TW: (24 + k) * TW + 1024],
                wsh_sp[k*128:(k+1)*128, :]), waits=wa, inc=s_w2[0], amt=16)
            wsh_toks.append(t)
        t_shb = p.op("sync", lambda e: e.dma_start(biasrow[:1, :], rows_sp[8:9, :]),
                     waits=[tok_biasrow_users], inc=s_br, amt=16)
        sh_mm_last = None
        for tt in range(8):
            for ch in range(2):
                bi = (tt * 2 + ch) % 4
                bank = BANKS[bi]
                w0 = [wsh_toks[-1], t_shb, bank_ev2[bi]]
                last = None
                for k in range(8):
                    last = p.op("tensor", lambda e, k=k, tt=tt, ch=ch, bank=bank: e.matmul(
                        bank[:, 0:512], xnw[:, k*TW + tt*128: k*TW + (tt+1)*128],
                        hT[:, (24+k)*TW + ch*512: (24+k)*TW + (ch+1)*512],
                        start=(k == 0), stop=False),
                        waits=w0 if k == 0 else (), inc=s_pe)
                    w0 = []
                last = p.op("tensor", lambda e, ch=ch, bank=bank: e.matmul(
                    bank[:, 0:512], ones128[:1, :], biasrow[:1, ch*512:(ch+1)*512],
                    start=False, stop=False), inc=s_pe)
                last = p.op("tensor", lambda e, ch=ch, bank=bank: e.matmul(
                    bank[:, 0:512], ones128[:1, :], biasrow[:1, D + ch*512: D + (ch+1)*512],
                    start=False, stop=True), inc=s_pe)
                sh_mm_last = last
                t_g = p.op("scalar", lambda e, bank=bank: e.activation(
                    gst[:, :], bank[:, 0:512], AF.Gelu),
                    waits=[last, gst_free[0]], inc=s_a)
                t_o = p.op("vector", lambda e, tt=tt, ch=ch: e.tensor_scalar(
                    outw[:, tt*TW + ch*512: tt*TW + (ch+1)*512], gst[:, :],
                    0.5, None, ALU.mult), waits=[t_g], inc=s_v)
                gst_free[0] = t_o
                bank_ev2[bi] = t_o
        tok_biasrow_users = sh_mm_last

        for ei in range(N_EXPERTS):
            mm1_last = None
            for ht in range(32):
                for ch in range(2):
                    bi = (ht * 2 + ch) % 4
                    bank = BANKS[bi]
                    w0 = [bank_ev2[bi]]
                    if ht == 0 and ch == 0:
                        w0.append(tok_w1_loaded)
                    last = None
                    for k in range(8):
                        last = p.op("tensor", lambda e, k=k, ht=ht, ch=ch, bank=bank: e.matmul(
                            bank[:, 0:512],
                            w1buf[:, k*HID + ht*128: k*HID + (ht+1)*128],
                            xnw[:, k*TW + ch*512: k*TW + (ch+1)*512],
                            start=(k == 0), stop=(k == 7)),
                            waits=w0 if k == 0 else (), inc=s_pe)
                        w0 = []
                    mm1_last = last
                    t_h = p.op("scalar", lambda e, ei=ei, ht=ht, ch=ch, bank=bank: e.activation(
                        hT[:, ht*TW + ch*512: ht*TW + (ch+1)*512], bank[:, 0:512],
                        AF.Gelu, bias=b1s[:, ei*32+ht: ei*32+ht+1]),
                        waits=[last], inc=s_a)
                    bank_ev2[bi] = t_h
            nxt = ei + 1
            if not (w == NW - 1 and ei == N_EXPERTS - 1):
                tok_w1_loaded = load_w1(nxt % N_EXPERTS, [mm1_last])
            tok_w1_users = mm1_last
            t_b2 = p.op("sync", lambda e, ei=ei: e.dma_start(
                biasrow[:1, :], rows_sp[ei:ei+1, :]),
                waits=[tok_biasrow_users], inc=s_br, amt=16)
            mm2_last = None
            ring_cons = {}
            def load_w2(hk, wa):
                slot = hk % 3
                off = (2048 * ei + 1024 + hk * 32) * HID
                return p.op("sync", lambda e, off=off, slot=slot: e.dma_start(
                    w2r[:, slot*1024:(slot+1)*1024],
                    AP(wag_out, off, [[1024, 128], [1, 1024]])),
                    waits=wa, inc=s_w2[slot], amt=16)
            for half in range(2):
                ring_toks = {}
                for hk in range(3):
                    ring_toks[hk] = load_w2(hk, [ring_cons.get(hk % 3)])
                for hk in range(32):
                    slot = hk % 3
                    last = None
                    for tt in range(4):
                        for ch in range(2):
                            bi = tt * 2 + ch
                            bank = BANKS[bi]
                            w0 = []
                            if tt == 0 and ch == 0:
                                w0.append(ring_toks[hk])
                            if hk == 0:
                                w0.append(bank_ev2[bi])
                            last = p.op("tensor", lambda e, hk=hk, tt=tt, ch=ch, half=half, bank=bank, slot=slot: e.matmul(
                                bank[:, 0:512],
                                hT[:, hk*TW + half*512 + tt*128: hk*TW + half*512 + (tt+1)*128],
                                w2r[:, slot*1024 + ch*512: slot*1024 + (ch+1)*512],
                                start=(hk == 0), stop=False),
                                waits=w0, inc=s_pe)
                    ring_cons[slot] = last
                    if hk + 3 < 32:
                        ring_toks[hk + 3] = load_w2(hk + 3, [last if hk == 29 else ring_cons.get((hk + 3) % 3)])
                for tt in range(4):
                    for ch in range(2):
                        bi = tt * 2 + ch
                        bank = BANKS[bi]
                        last = p.op("tensor", lambda e, ch=ch, bank=bank: e.matmul(
                            bank[:, 0:512], ones128[:1, :],
                            biasrow[:1, ch*512:(ch+1)*512],
                            start=False, stop=False), waits=[t_b2] if tt == 0 and ch == 0 else (), inc=s_pe)
                        last = p.op("tensor", lambda e, ch=ch, bank=bank: e.matmul(
                            bank[:, 0:512], ones128[:1, :],
                            biasrow[:1, D + ch*512: D + (ch+1)*512],
                            start=False, stop=True), inc=s_pe)
                        mm2_last = last
                        gtt = half * 4 + tt
                        ccol = (w * 8 + gtt) * E + ei
                        t_e = p.op("vector", lambda e, gtt=gtt, ch=ch, ccol=ccol, bank=bank: e.scalar_tensor_tensor(
                            outw[:, gtt*TW + ch*512: gtt*TW + (ch+1)*512],
                            bank[:, 0:512], cmb[:, ccol:ccol+1],
                            outw[:, gtt*TW + ch*512: gtt*TW + (ch+1)*512],
                            ALU.mult, ALU.add), waits=[last], inc=s_v)
                        bank_ev2[bi] = t_e
            tok_biasrow_users = mm2_last
            tok_hT_users = mm2_last

        ln_last = None
        for tt in range(8):
            ow = lambda a, b: outw[:, tt*TW + a: tt*TW + b]
            t1 = p.op("vector", lambda e, tt=tt: e.reduce_sum(
                stat[:, 0:1], outw[:, tt*TW:(tt+1)*TW], AX.X),
                waits=[bank_ev2[(tt*2) % 8], bank_ev2[(tt*2+1) % 8]], inc=s_v)
            t2 = p.op("vector", lambda e: e.tensor_scalar(
                stat[:, 1:2], stat[:, 0:1], 1.0 / D, None, ALU.mult), waits=[t1], inc=s_v)
            t3 = p.op("vector", lambda e, tt=tt: e.tensor_scalar(
                outw[:, tt*TW:(tt+1)*TW], outw[:, tt*TW:(tt+1)*TW],
                stat[:, 1:2], None, ALU.subtract), waits=[t2], inc=s_v)
            t4 = p.op("scalar", lambda e, tt=tt: e.square(
                lnq[:, :], outw[:, tt*TW:(tt+1)*TW]), waits=[t3], inc=s_a)
            t5 = p.op("vector", lambda e: e.reduce_sum(stat[:, 2:3], lnq[:, :], AX.X),
                      waits=[t4], inc=s_v)
            t6 = p.op("vector", lambda e: e.tensor_scalar(
                stat[:, 3:4], stat[:, 2:3], 1.0 / D, 1e-5, ALU.mult, ALU.add),
                waits=[t5], inc=s_v)
            t7 = p.op("scalar", lambda e: e.sqrt(stat[:, 4:5], stat[:, 3:4]),
                      waits=[t6], inc=s_a)
            t8 = p.op("vector", lambda e: e.reciprocal(stat[:, 5:6], stat[:, 4:5]),
                      waits=[t7], inc=s_v)
            t9 = p.op("vector", lambda e, tt=tt: e.tensor_scalar(
                outw[:, tt*TW:(tt+1)*TW], outw[:, tt*TW:(tt+1)*TW],
                stat[:, 5:6], None, ALU.mult), waits=[t8], inc=s_v)
            t10 = p.op("vector", lambda e, tt=tt: e.tensor_mul(
                outw[:, tt*TW:(tt+1)*TW], outw[:, tt*TW:(tt+1)*TW], gam_bc[:, :]),
                waits=[t9], inc=s_v)
            t11 = p.op("vector", lambda e, tt=tt: e.tensor_add(
                outw[:, tt*TW:(tt+1)*TW], outw[:, tt*TW:(tt+1)*TW], bet_bc[:, :]),
                waits=[t10], inc=s_v)
            ln_last = t11
        t_sp = p.op("scalar", lambda e, w=w: e.dma_start(
            ffn_sp[w*TW:(w+1)*TW, :].rearrange("(t p) d -> p t d", p=128),
            outw[:, :].rearrange("p (t d) -> p t d", t=8)),
            waits=[ln_last], inc=s_ad, amt=16)
        spill_toks[w] = t_sp
        tok_outw_spill = t_sp
        if w + 1 < NW:
            xnw_toks = []
            for k in range(8):
                wa = []
                if k == 0:
                    wa = [mm2_last]
                t = p.op("sync", lambda e, k=k, w=w: e.dma_start(
                    xnw[:, k*TW:(k+1)*TW],
                    xnT_sp[k*128:(k+1)*128, (w+1)*TW:(w+2)*TW]),
                    waits=wa, inc=s_xw, amt=16)
                xnw_toks.append(t)
            tok_xnw_loaded = xnw_toks[-1]

    FFN_DONE = [spill_toks[-1]]
    free("lnq", "gst", "outw", "hT", "w2r", "w1buf", "xnw")
    pf_cm[1].__exit__(None, None, None)
    pf_cm[0].__exit__(None, None, None)
    pt_cm2 = [nc.psum_tensor(f"pt{i}b", [128, 1024], BF16) for i in range(2)]
    pt[0] = pt_cm2[0].__enter__()
    pt[1] = pt_cm2[1].__enter__()

    qsec = [alloc(f"qsec{t}", [128, 2 * SEC], BF16) for t in range(2)]
    ksec = [alloc(f"ksec{t}", [128, 2 * SEC], BF16) for t in range(2)]
    vsec = [alloc(f"vsec{t}", [128, 2 * SEC], BF16) for t in range(2)]
    bsec = alloc("bsec", [1, 4 * 2 * SEC], BF16)
    otsec = [alloc(f"otsec{t}", [128, SEC], BF16) for t in range(2)]
    knT_t = [alloc(f"knTt{h}", [128, 128], BF16) for h in range(2)]
    wt_t = [alloc(f"wtt{h}", [128, 128], BF16) for h in range(2)]
    st16_t = [alloc(f"st16t{h}", [128, DH], BF16) for h in range(2)]
    st32_t = [alloc(f"st32t{h}", [128, DH], F32) for h in range(2)]
    wk = {}
    for j in range(4):
        pb = 64 * (j % 2)
        wk[j] = dict(
            k16=alloc(f"k16_{j}", [128, DH], BF16),
            kn=alloc(f"kn_{j}", [128, DH], BF16),
            knT=knT_t[j // 2][pb:pb+64, :],
            wt=wt_t[j // 2][pb:pb+64, :],
            v16=alloc(f"v16_{j}", [128, DH], BF16),
            L=alloc(f"L_{j}", [128, 128], BF16),
            P=[alloc(f"P{k}_{j}", [128, 128], BF16) for k in range(NPOW + 1)],
            R=alloc(f"R_{j}", [128, 2 * 128], BF16),
            y=alloc(f"y_{j}", [128, DH], BF16),
            u=alloc(f"u_{j}", [128, DH], BF16),
            at=alloc(f"at_{j}", [128, 128], BF16),
            yk=alloc(f"yk_{j}", [128, DH], BF16),
            sqk=alloc(f"sqk_{j}", [128, DH], F32),
            o16=alloc(f"o16_{j}", [128, DH], BF16),
            st32=st32_t[j // 2][pb:pb+64, :],
            st16=st16_t[j // 2][pb:pb+64, :],
            bv=alloc(f"bv_{j}", [128, 1], F32),
            sq=alloc(f"sq_{j}", [128, DH], F32),
        )

    st_init = []
    for j in range(4):
        t1 = p.op("vector", lambda e, j=j: e.memset(wk[j]["st32"], 0.0),
                  waits=(P1_DONE + FFN_DONE) if j == 0 else (), inc=s_v)
        t2 = p.op("vector", lambda e, j=j: e.memset(wk[j]["st16"], 0.0),
                  waits=[t1], inc=s_v)
        st_init.append(t2)
    tok_p2_start = st_init[-1]

    sec_cons = {}
    pair_state = {j: {"st": st_init[j], "stq": None} for j in range(4)}
    secbuf = {0: None, 1: None, 2: None}

    wdone_waits = [("s_w1", p.counts[s_w1])] + [
        (f"s_w2_{i}", p.counts[s_w2[i]]) for i in range(3)] + [
        ("s_xw", p.counts[s_xw]), ("s_br", p.counts[s_br])]

    def emit_section_loads(s):
        sl = s % 2
        waits = list(consol_toks) + list(FFN_DONE) + wdone_waits
        if s >= 2:
            waits = [sec_cons[(jj, s - 2)] for jj in range(4)]
        tens3 = (qsec, ksec, vsec)
        toks = []
        for sel in range(3):
            for half in range(2):
                t = p.op("sync", lambda e, sel=sel, half=half, s=s, sl=sl: e.dma_start(
                    tens3[sel][half][:, sl*SEC:(sl+1)*SEC],
                    qkv_my[(sel*2+half)*128:(sel*2+half+1)*128, s*SEC:(s+1)*SEC]),
                    waits=waits if (sel == 0 and half == 0) else (),
                    inc=s_sec[sl], amt=16)
                toks.append(t)
        for j in range(4):
            t = p.op("sync", lambda e, j=j, s=s, sl=sl: e.dma_start(
                bsec[:1, (j * 2 + sl) * SEC:(j * 2 + sl + 1) * SEC],
                beta_my[j:j+1, s*SEC:(s+1)*SEC]), inc=s_sec[sl], amt=16)
            toks.append(t)
        return toks

    PAIR_BANKS = {0: [0, 1], 1: [2, 3], 2: [4], 3: [5]}
    ptev = {0: bank_ev2[6], 1: bank_ev2[7]}
    psev = {i: bank_ev2[i] for i in range(6)}
    ps_rr = {j: 0 for j in range(4)}

    def tr_op(j, in_ap, rI, cI, evac_fn, eng2, waits=(), bp=0):
        bk = j // 2
        bank = pt[bk]
        w = list(waits)
        if ptev[bk] is not None:
            w.append(ptev[bk])
        t_tr = p.op("tensor", lambda e, in_ap=in_ap, bank=bank, rI=rI, cI=cI, bp=bp: e.transpose(
            bank[0:cI, 0:rI], in_ap, ident[bp:bp+rI, bp:bp+rI]), waits=w, inc=s_pe)
        t_ev = p.op(eng2, lambda e, bank=bank, rI=rI, cI=cI, evac_fn=evac_fn: evac_fn(
            e, bank[0:cI, 0:rI]), waits=[t_tr], inc=s_v if eng2 == "vector" else s_a)
        ptev[bk] = t_ev
        return t_ev

    def ps_op(j, mm_fns, evac_fn, eng2, waits=()):
        blist = PAIR_BANKS[j]
        bi = blist[ps_rr[j] % len(blist)]
        ps_rr[j] += 1
        bank = ps[bi]
        w = list(waits)
        if psev[bi] is not None:
            w.append(psev[bi])
        last = None
        for ix, f in enumerate(mm_fns):
            last = p.op("tensor", lambda e, f=f, bank=bank: f(e, bank),
                        waits=w if ix == 0 else (), inc=s_pe)
        t_ev = p.op(eng2, lambda e, bank=bank, evac_fn=evac_fn: evac_fn(e, bank),
                    waits=[last], inc=s_v if eng2 == "vector" else s_a)
        psev[bi] = t_ev
        return t_ev, last

    ot_store_toks = []
    ot_sec_last = {0: None, 1: None}
    ot_written = {0: None, 1: None}

    def chunk_gen(ci, j, sl, sec_tok_count):
        W = wk[j]
        cc = (ci % CPS) * CH
        qco = sl * SEC + cc
        sec_wait = (s_sec[sl], sec_tok_count)
        pb = 64 * (j % 2)
        tq, tk, tv = qsec[j // 2], ksec[j // 2], vsec[j // 2]
        stt = pair_state[j]

        t_k16 = tr_op(j, tk[pb:pb+64, qco:qco+CH], 64, CH,
                      lambda e, b: e.tensor_copy(W["k16"][:, :], b[0:128, 0:DH]),
                      "vector", waits=[sec_wait, stt.get("sq")], bp=pb)
        t_v16 = tr_op(j, tv[pb:pb+64, qco:qco+CH], 64, CH,
                      lambda e, b: e.copy(W["v16"][:, :], b[0:128, 0:DH]),
                      "scalar", waits=[sec_wait, stt.get("yv0")], bp=pb)
        bcol = (j * 2 + sl) * SEC + cc
        t_bv, _ = ps_op(j, [lambda e, b, bcol=bcol: e.matmul(
            b[0:128, 0:1], bsec[:1, bcol:bcol+CH], ones128[:1, 0:1],
            start=True, stop=True)],
            lambda e, b: e.activation(W["bv"][:, :], b[0:128, 0:1], AF.Sigmoid),
            "scalar", waits=[sec_wait])
        yield
        c0, c1, c2 = 12 + 4 * j, 13 + 4 * j, 14 + 4 * j
        t_sq = p.op("scalar", lambda e: e.square(W["sq"][:, :], W["k16"][:, :]),
                    waits=[t_k16, stt.get("ss")], inc=s_a)
        t_ss = p.op("vector", lambda e, c0=c0: e.reduce_sum(
            stat[:, c0:c0+1], W["sq"][:, :], AX.X), waits=[t_sq], inc=s_v)
        stt["ss"] = t_ss
        stt["sq"] = t_sq
        t_sd = p.op("scalar", lambda e, c0=c0, c1=c1: e.sqrt(
            stat[:, c1:c1+1], stat[:, c0:c0+1]), waits=[t_ss], inc=s_a)
        t_mx = p.op("vector", lambda e, c1=c1: e.tensor_scalar(
            stat[:, c1:c1+1], stat[:, c1:c1+1], 1e-12, None, ALU.max),
            waits=[t_sd], inc=s_v)
        t_rc = p.op("vector", lambda e, c1=c1, c2=c2: e.reciprocal(
            stat[:, c2:c2+1], stat[:, c1:c1+1]), waits=[t_mx], inc=s_v)
        t_kn = p.op("vector", lambda e, c2=c2: e.tensor_scalar(
            W["kn"][:, :], W["k16"][:, :], stat[:, c2:c2+1], None, ALU.mult),
            waits=[t_rc], inc=s_v)
        yield
        t_knT = tr_op(j, W["kn"][:, :], 128, 64,
                      lambda e, b: e.tensor_copy(W["knT"], b[0:64, 0:128]),
                      "vector", waits=[t_kn])
        yield
        t_L, _ = ps_op(j, [lambda e, b: e.matmul(
            b[0:128, 0:128], W["knT"], W["knT"], start=True, stop=True)],
            lambda e, b: e.scalar_tensor_tensor(
                W["L"][:, :], b[0:128, 0:128], W["bv"][:, 0:1], maskLT[:, :],
                ALU.mult, ALU.mult),
            "vector", waits=[t_knT, t_bv])
        yield
        t_P0 = tr_op(j, W["L"][:, :], 128, 128,
                     lambda e, b: e.copy(W["P"][0][:, :], b[0:128, 0:128]),
                     "scalar", waits=[t_L])
        yield
        t_prevP = t_P0
        t_prevR = t_L
        prevR_ap = W["L"][:, :]
        for kq in range(1, NPOW + 1):
            rsl = kq % 2
            t_R, _ = ps_op(j, [lambda e, b, kq=kq, prevR_ap=prevR_ap: e.matmul(
                b[0:128, 0:128], W["P"][kq - 1][:, :], prevR_ap,
                start=True, stop=True)],
                lambda e, b, rsl=rsl: e.tensor_copy(
                    W["R"][:, rsl*128:(rsl+1)*128], b[0:128, 0:128]),
                "vector", waits=[t_prevP, t_prevR])
            yield
            t_P = tr_op(j, W["R"][:, rsl*128:(rsl+1)*128], 128, 128,
                        (lambda kq: lambda e, b: e.copy(
                            W["P"][kq][:, :], b[0:128, 0:128]))(kq),
                        "scalar", waits=[t_R])
            t_prevP = t_P
            t_prevR = t_R
            prevR_ap = W["R"][:, rsl*128:(rsl+1)*128]
            yield
        t_y0, _ = ps_op(j, [lambda e, b: e.matmul(
            b[0:128, 0:DH], W["knT"], W["st16"], start=True, stop=True)],
            lambda e, b: e.tensor_sub(W["sq"][:, :], W["v16"][:, :], b[0:128, 0:DH]),
            "vector", waits=[t_knT, t_v16, stt["st"]])
        t_y = p.op("vector", lambda e: e.tensor_scalar(
            W["y"][:, :], W["sq"][:, :], W["bv"][:, 0:1], None, ALU.mult),
            waits=[t_y0, t_bv], inc=s_v)
        yield
        for kq in range(NPOW, 0, -1):
            t_y, _ = ps_op(j, [(lambda kq: lambda e, b: e.matmul(
                b[0:128, 0:DH], W["P"][kq][:, :], W["y"][:, :],
                start=True, stop=True))(kq)],
                lambda e, b: e.tensor_add(W["y"][:, :], W["y"][:, :], b[0:128, 0:DH]),
                "vector", waits=[t_y])
            yield
        t_u, _ = ps_op(j, [lambda e, b: e.matmul(
            b[0:128, 0:DH], W["P"][0][:, :], W["y"][:, :], start=True, stop=True)],
            lambda e, b: e.tensor_sub(W["u"][:, :], W["y"][:, :], b[0:128, 0:DH]),
            "vector", waits=[t_y])
        yield
        t_at, _ = ps_op(j, [lambda e, b: e.matmul(
            b[0:128, 0:128], W["knT"], tq[pb:pb+64, qco:qco+CH],
            start=True, stop=True)],
            lambda e, b: e.scalar_tensor_tensor(
                W["at"][:, :], b[0:128, 0:128], 1.0, maskUT[:, :],
                ALU.mult, ALU.mult),
            "vector", waits=[t_knT, sec_wait])
        yield
        t_o, _ = ps_op(j, [
            lambda e, b: e.matmul(b[0:128, 0:DH], tq[pb:pb+64, qco:qco+CH],
                                  W["st16"], start=True, stop=False),
            lambda e, b: e.matmul(b[0:128, 0:DH], W["at"][:, :], W["u"][:, :],
                                  start=False, stop=True)],
            lambda e, b: e.copy(W["o16"][:, :], b[0:128, 0:DH]),
            "scalar", waits=[t_at, t_u, stt["st"]])
        yield
        w_ot = [t_o]
        if (ci % CPS) == 0 and ot_written[j // 2] is not None:
            w_ot.append(ot_written[j // 2])
        t_ot = tr_op(j, W["o16"][:, :], 128, 64,
                     (lambda j=j, cc=cc: lambda e, b: e.tensor_copy(
                         otsec[j // 2][64*(j % 2):64*(j % 2)+64, cc:cc+CH],
                         b[0:64, 0:128]))(),
                     "vector", waits=w_ot)
        ot_sec_last[j // 2] = t_ot
        t_stp, _ = ps_op(j, [lambda e, b: e.matmul(
            b[0:64, 0:DH], W["kn"][:, :], W["u"][:, :], start=True, stop=True)],
            lambda e, b: e.tensor_add(W["st32"], W["st32"], b[0:64, 0:DH]),
            "vector", waits=[t_u, stt["stq"]])
        t_st16 = p.op("scalar", lambda e: e.copy(W["st16"], W["st32"]),
                      waits=[t_stp, t_o], inc=s_a)
        stt["st"] = t_st16
        stt["stq"] = t_stp
        sec_cons[(j, ci // CPS)] = t_o

    emit_section_loads(0)
    emit_section_loads(1)
    for s in range(NSEC):
        sl = s % 2
        need = 16 * 10 * (s // 2 + 1)
        for cl in range(CPS):
            ci = s * CPS + cl
            gens = [chunk_gen(ci, j, sl, need) for j in range(4)]
            done = [False] * 4
            while not all(done):
                for gi, g in enumerate(gens):
                    if not done[gi]:
                        try:
                            next(g)
                        except StopIteration:
                            done[gi] = True
        for half in range(2):
            t_w = p.op("gpsimd", lambda e, half=half, s=s: e.dma_start(
                oag_in[half*128:(half+1)*128, s*SEC:(s+1)*SEC], otsec[half][:, :]),
                waits=[ot_sec_last[half]], inc=s_g, amt=16)
            ot_written[half] = t_w
            ot_store_toks.append(t_w)
        if s + 2 < NSEC:
            emit_section_loads(s + 2)
    tok_o_ag = p.op("gpsimd", lambda e: e.collective_compute(
        "AllGather", ALU.bypass, replica_groups=RG,
        ins=[oag_in.ap().opt()], outs=[oag_out.ap().opt()]),
        waits=ot_store_toks[-2:], inc=s_g)

    P2_DONE = [tok_o_ag] + [pair_state[j]["st"] for j in range(4)]

    if dbg == "p2":
        _dbg_p2(nc, p, locals())
        return nc

    def c_o(e):
        return e.dma_start(o_my[:, :], AP(oag_out, dynregs["o"], [[T, 1024], [1, TOK]]))
    tok_o_my0 = p.op("gpsimd", c_o, waits=[tok_o_ag], inc=s_g, amt=16)
    tok_o_my = p.op("gpsimd", c_o, waits=[tok_o_my0], inc=s_g, amt=16)

    free_to(mk_p2)
    for i in range(6):
        bank_ev2[i] = psev[i]
    otb = alloc("otb", [128, 2 * 8 * 512], BF16)
    wout16 = alloc("wout16", [128, 8 * 1024], BF16)
    frr = alloc("frr", [128, 2 * 1024], F32)
    xrr = alloc("xrr", [128, 2 * 1024], F32)
    ostage = alloc("ostage", [128, 1024], F32)
    s_out = p.sem("s_out")
    s_otb = [p.sem("s_otb0"), p.sem("s_otb1")]

    wout_toks = []
    for k in range(8):
        t = p.op("sync", lambda e, k=k: e.dma_start(
            wout16[:, k*1024:(k+1)*1024], wout_sp[k*128:(k+1)*128, :]),
            waits=(FFN_DONE + P2_DONE) if k == 0 else (), inc=s_w1, amt=16)
        wout_toks.append(t)
    tok_wout = wout_toks[-1]

    otb_cons = [None, None]
    xr_cons = [None, None]
    fr_cons = [None, None]
    ostage_free = [None]
    out_cnt = [0]
    tok_out_last = None
    for st in range(8):
        sl = st % 2
        ot_toks = []
        for k in range(8):
            wa = []
            if k == 0:
                wa = [tok_o_my, otb_cons[sl]]
            t = p.op("sync", lambda e, k=k, st=st, sl=sl: e.dma_start(
                otb[:, sl*4096 + k*512: sl*4096 + (k+1)*512],
                o_my[k*128:(k+1)*128, st*512:(st+1)*512]),
                waits=wa, inc=s_otb[sl], amt=16)
            ot_toks.append(t)
        for ti in range(4):
            tt = st * 4 + ti
            t_x = p.op("sync", lambda e, tt=tt, ti=ti: e.dma_start(
                xrr[:, (tt % 2)*D:(tt % 2 + 1)*D], x_my[tt*128:(tt+1)*128, :]),
                waits=[xr_cons[tt % 2]], inc=s_xr[tt % 2], amt=16)
            t_f = p.op("sync", lambda e, tt=tt: e.dma_start(
                frr[:, (tt % 2)*D:(tt % 2 + 1)*D], ffn_sp[tt*128:(tt+1)*128, :]),
                waits=[spill_toks[tt // 8], fr_cons[tt % 2]], inc=s_fr[tt % 2], amt=16)
            evs = []
            for ch in range(2):
                bi = (tt * 2 + ch) % 4
                bank = BANKS[bi]
                w0 = [ot_toks[-1], bank_ev2[bi]]
                if ti == 0 and ch == 0:
                    w0.append(tok_wout)
                last = None
                for k in range(8):
                    last = p.op("tensor", lambda e, k=k, ti=ti, ch=ch, sl=sl, bank=bank: e.matmul(
                        bank[:, 0:512],
                        otb[:, sl*4096 + k*512 + ti*128: sl*4096 + k*512 + (ti+1)*128],
                        wout16[:, k*1024 + ch*512: k*1024 + (ch+1)*512],
                        start=(k == 0), stop=(k == 7)),
                        waits=w0 if k == 0 else (), inc=s_pe)
                    w0 = []
                t_s1 = p.op("vector", lambda e, ch=ch, tt=tt, bank=bank: e.tensor_add(
                    ostage[:, ch*512:(ch+1)*512], bank[:, 0:512],
                    xrr[:, (tt % 2)*D + ch*512: (tt % 2)*D + (ch+1)*512]),
                    waits=[last, t_x, ostage_free[0]], inc=s_v)
                t_s2 = p.op("vector", lambda e, ch=ch, tt=tt: e.tensor_add(
                    ostage[:, ch*512:(ch+1)*512], ostage[:, ch*512:(ch+1)*512],
                    frr[:, (tt % 2)*D + ch*512: (tt % 2)*D + (ch+1)*512]),
                    waits=[t_f], inc=s_v)
                bank_ev2[bi] = t_s2
                evs.append(t_s2)
            xr_cons[tt % 2] = evs[-1]
            fr_cons[tt % 2] = evs[-1]
            t_w = p.op("scalar", lambda e, tt=tt: e.dma_start(
                out_my[tt*128:(tt+1)*128, :], ostage[:, :]), waits=evs, inc=s_out, amt=16)
            ostage_free[0] = t_w
            tok_out_last = t_w
        otb_cons[sl] = evs[-1]

    _finalize(nc, p, [(s_out, p.counts[s_out])])
    return nc


def _emit_ffn(nc, p, S):
    raise NotImplementedError


def _emit_p3(nc, p, S, ffn_sec):
    raise NotImplementedError


def _dbg_p1(nc, p, S):
    alloc = S["alloc"]
    S["free"]("gl_u", "pstage", "xn16", "ln_sq", "ln_xc", "cast16", "cast_in", "xring")
    s_v, s_g = S["s_v"], S["s_g"]
    qkvag_in, wag_in, xnT, cmb = S["qkvag_in"], S["wag_in"], S["xnT"], S["cmb"]
    out_my = S["out_my"]
    d_qkv = nc.dram_tensor("d_qkv", [QKV_ROWS, TOK], F32, kind="ExternalOutput")
    d_xnT = nc.dram_tensor("d_xnT", [D, TOK], F32, kind="ExternalOutput")
    d_cmb = nc.dram_tensor("d_cmb", [128, NXT * E], F32, kind="ExternalOutput")
    d_wag = nc.dram_tensor("d_wag", [2048, HID], F32, kind="ExternalOutput")
    cvt = alloc("cvt", [128, TOK], F32)
    cvt16 = alloc("cvt16", [128, TOK], BF16)
    prevs = list(S["P1_DONE"])
    for r in range(25):
        rows = 128 if r < 24 else 16
        r0 = r * 128 if r < 24 else 3072
        t1 = p.op("gpsimd", lambda e, r0=r0, rows=rows: e.dma_start(
            cvt16[0:rows, :], qkvag_in[r0:r0+rows, :]), waits=prevs, inc=s_g, amt=16)
        t2 = p.op("vector", lambda e, rows=rows: e.tensor_copy(
            cvt[0:rows, :], cvt16[0:rows, :]), waits=[t1], inc=s_v)
        t3 = p.op("gpsimd", lambda e, r0=r0, rows=rows: e.dma_start(
            d_qkv[r0:r0+rows, :], cvt[0:rows, :]), waits=[t2], inc=s_g, amt=16)
        prevs = [t3]
    for k in range(8):
        t2 = p.op("vector", lambda e, k=k: e.tensor_copy(
            cvt[:, :], S["xnT"][:, k*TOK:(k+1)*TOK]), waits=prevs, inc=s_v)
        t3 = p.op("gpsimd", lambda e, k=k: e.dma_start(
            d_xnT[k*128:(k+1)*128, :], cvt[:, :]), waits=[t2], inc=s_g, amt=16)
        prevs = [t3]
    t3 = p.op("gpsimd", lambda e: e.dma_start(d_cmb[:, :], cmb[:, :]),
              waits=prevs, inc=s_g, amt=16)
    prevs = [t3]
    for r in range(16):
        t1 = p.op("gpsimd", lambda e, r=r: e.dma_start(
            cvt16[:, 0:HID], wag_in[r*128:(r+1)*128, :]), waits=prevs, inc=s_g, amt=16)
        t2 = p.op("vector", lambda e: e.tensor_copy(
            cvt[:, 0:HID], cvt16[:, 0:HID]), waits=[t1], inc=s_v)
        t3 = p.op("gpsimd", lambda e, r=r: e.dma_start(
            d_wag[r*128:(r+1)*128, :], cvt[:, 0:HID]), waits=[t2], inc=s_g, amt=16)
        prevs = [t3]
    t_out = p.op("gpsimd", lambda e: e.dma_start(out_my[:, :], S["x_my"][:, :]),
                 waits=prevs, inc=s_g, amt=16)
    _finalize(nc, p, [t_out])


def _dbg_p2(nc, p, S):
    alloc = S["alloc"]
    s_v, s_g = S["s_v"], S["s_g"]
    oag_in = S["oag_in"]
    out_my = S["out_my"]
    d_ot = nc.dram_tensor("d_ot", [NH * DH, T], F32, kind="ExternalOutput")
    cvt = alloc("cvt2", [128, TOK], F32)
    cvt16 = alloc("cvt216", [128, TOK], BF16)
    prevs = list(S["P2_DONE"])
    for r in range(2):
        for cb in range(4):
            t1 = p.op("gpsimd", lambda e, r=r, cb=cb: e.dma_start(
                cvt16[:, :], oag_in[r*128:(r+1)*128, cb*TOK:(cb+1)*TOK]),
                waits=prevs, inc=s_g, amt=16)
            t2 = p.op("vector", lambda e: e.tensor_copy(cvt[:, :], cvt16[:, :]),
                      waits=[t1], inc=s_v)
            t3 = p.op("gpsimd", lambda e, r=r, cb=cb: e.dma_start(
                d_ot[r*128:(r+1)*128, cb*TOK:(cb+1)*TOK], cvt[:, :]),
                waits=[t2], inc=s_g, amt=16)
            prevs = [t3]
    t_out = p.op("gpsimd", lambda e: e.dma_start(out_my[:, :], S["x_my"][:, :]),
                 waits=prevs, inc=s_g, amt=16)
    _finalize(nc, p, [t_out])


def _finalize(nc, plan, final_toks):
    plan.resolve()
    with nc.Block() as block:
        sems = {}
        for name in plan.sem_names:
            cm = nc.semaphore(name)
            sems[name] = cm.__enter__()

        @block.sync
        def _(e):
            _emit_engine(plan, sems, "sync", e)

        @block.tensor
        def _(e):
            _emit_engine(plan, sems, "tensor", e)

        @block.vector
        def _(e):
            _emit_engine(plan, sems, "vector", e)

        @block.scalar
        def _(e):
            _emit_engine(plan, sems, "scalar", e)

        @block.gpsimd
        def _(e):
            _emit_engine(plan, sems, "gpsimd", e)
            for w in final_toks:
                if isinstance(w, dict):
                    e.wait_ge(sems[w["sem"]], w["val"])
                else:
                    sname, v = w
                    e.wait_ge(sems[sname], v)


def _emit_engine(plan, sems, ename, e):
    for rec in plan.ops[ename]:
        for w in rec["waits"]:
            if isinstance(w, dict):
                assert w.get("keep"), f"wait on dropped token {w['sem']}"
                e.wait_ge(sems[w["sem"]], w["val"])
            else:
                sname, v = w
                e.wait_ge(sems[sname], v)
        inst = rec["fn"](e)
        rec["inst"] = inst
        if rec["sem"] is not None and (rec["amt"] != 1 or rec["used"]):
            inst.then_inc(sems[rec["sem"]], rec["amt"])


def make_inmaps(inputs):
    x = np.ascontiguousarray(np.asarray(inputs["L0_attn_x"], np.float32).reshape(B * T, D))
    Wq = np.asarray(inputs["L0_attn_Wq"], np.float32)
    Wk = np.asarray(inputs["L0_attn_Wk"], np.float32)
    Wv = np.asarray(inputs["L0_attn_Wv"], np.float32)
    wqkv = np.ascontiguousarray(np.concatenate([Wq, Wk, Wv], axis=1))
    wbeta = np.ascontiguousarray(np.asarray(inputs["L0_attn_Wbeta"], np.float32))
    b_beta_col = np.ascontiguousarray(
        np.asarray(inputs["L0_attn_b_beta"], np.float32).reshape(16, 1))
    woutm = np.ascontiguousarray(np.asarray(inputs["L0_attn_Wout"], np.float32))
    W1 = np.asarray(inputs["L0_ffn_W1"], np.float32)
    b1 = np.asarray(inputs["L0_ffn_b1"], np.float32)
    W2 = np.asarray(inputs["L0_ffn_W2"], np.float32)
    b2 = np.asarray(inputs["L0_ffn_b2"], np.float32)
    wgate = np.ascontiguousarray(np.asarray(inputs["L0_ffn_gate_W"], np.float32))
    wshm = np.ascontiguousarray(np.asarray(inputs["L0_ffn_shared_W"], np.float32))
    shbm = np.asarray(inputs["L0_ffn_shared_b"], np.float32).reshape(1, D)
    gam = np.asarray(inputs["L0_ffn_gamma"], np.float32).reshape(1, D)
    bet = np.asarray(inputs["L0_ffn_beta"], np.float32).reshape(1, D)
    b1_all = np.ascontiguousarray(b1.reshape(E, 32, 128).transpose(2, 0, 1).reshape(128, E * 32))
    b2_all = np.ascontiguousarray(b2)
    ident = np.eye(128, dtype=ml_dtypes.bfloat16)
    maskLT = np.tril(np.ones((128, 128), np.float32), k=-1)
    maskUT = np.triu(np.ones((128, 128), np.float32), k=0)
    ones = np.ones((1, 128), ml_dtypes.bfloat16)
    idenf = np.eye(128, dtype=np.float32)
    maps = []
    for c in range(8):
        wshard = np.ascontiguousarray(
            np.concatenate([W1[c], W2[c].reshape(1024, HID)], axis=0))
        maps.append(dict(
            x_my=np.ascontiguousarray(x[c*TOK:(c+1)*TOK]),
            wqkv=wqkv, wbeta=wbeta, b_beta_col=b_beta_col, wout=woutm, wsh=wshm,
            shb=shbm, wgate=wgate, gamma=gam, beta2=bet, wshard=wshard,
            b1_all=b1_all, b2_all=b2_all,
            ident_in=ident, maskLT_in=maskLT, maskUT_in=maskUT, ones_in=ones,
            idenf_in=idenf,
        ))
    return maps


def kernel(**inputs):
    nc = build()
    maps = make_inmaps(inputs)
    res = run_bass_kernel_spmd(nc, maps, core_ids=list(range(8)))
    out = np.concatenate([res.results[c]["out_my"] for c in range(8)], axis=0)
    return out.reshape(B, T, D)
